# version 18
# baseline (speedup 1.0000x reference)
import os
os.environ.setdefault("TRN_ALLOW_ICI_ALL_TO_ALL", "1")

import numpy as np
from contextlib import ExitStack

import concourse.bass as bass
import concourse.mybir as mybir
from concourse.bass_utils import run_bass_kernel_spmd

F32 = mybir.dt.float32
BF16 = mybir.dt.bfloat16
AF = mybir.ActivationFunctionType
ALU = mybir.AluOpType
AX = mybir.AxisListType

NC = 8
D = 2048
DC = 16
TPC = 1024
LT = 2
TOK = 512
NB = 2
NST = 8
DTR = 256
CH = 256
HB = DTR + 16
QT = 1024
EPS = 1e-5


def build_program():
    TRUNC = int(os.environ.get("KTRUNC", "9"))
    nc = bass.Bass("TRN2", target_bir_lowering=False)

    def din(name, shape, dt=F32):
        return nc.dram_tensor(name, shape, dt, kind="ExternalInput").ap()

    x0_d = din("x0", [LT, DC, 128, TOK])
    wb_d = din("wbT", [DC, 128, D])
    bb_d = din("bb", [128, DC])
    rt_d = din("routerT", [DC, 128, NST])
    at_d = din("AT", [DC, 128, 128])
    bw_d = din("BwT", [128, D])
    wx_d = din("WxpT", [DC, 128, 384])
    wdt_d = din("WdtT", [2, 128, CH])
    bdt_d = din("bdt", [128, 2])
    alog_d = din("Alog", [128, 2, NST])
    dpar_d = din("Dpar", [128, DC])
    ones_col_d = din("ones_col", [128, 128])
    ones_row_d = din("ones_row", [1, TOK])
    eye_d = din("eye", [128, 128])
    e16_d = din("E16", [NST, 128])

    out_d = nc.dram_tensor("out", [LT, DC, 128, TOK], F32, kind="ExternalOutput").ap()

    ag_inT = nc.dram_tensor("ag_inT", [NB, HB, TOK], BF16, kind="Internal").ap()
    ag_outT = nc.dram_tensor("ag_outT", [NB, NC, HB, TOK], BF16, kind="Internal").ap()
    a1_inT = nc.dram_tensor("a1_inT", [NB, NC, CH, TOK], F32, kind="Internal").ap()
    a1_outT = nc.dram_tensor("a1_outT", [NB, NC, CH, TOK], F32, kind="Internal").ap()
    a2_inT = nc.dram_tensor("a2_inT", [NB, NC, CH, TOK], F32, kind="Internal").ap()
    a2_outT = nc.dram_tensor("a2_outT", [NB, NC, CH, TOK], F32, kind="Internal").ap()
    ag_in = [ag_inT[b] for b in range(NB)]
    ag_out = [ag_outT[b] for b in range(NB)]
    a1_in = [a1_inT[b] for b in range(NB)]
    a1_out = [a1_outT[b] for b in range(NB)]
    a2_in = [a2_inT[b] for b in range(NB)]
    a2_out = [a2_outT[b] for b in range(NB)]

    T, V, S, G, Q = nc.tensor, nc.vector, nc.scalar, nc.gpsimd, nc.sync
    ENG = {"t": T, "v": V, "s": S, "g": G, "q": Q}

    es = ExitStack()
    sb = lambda n, s, d=F32: es.enter_context(nc.sbuf_tensor(n, s, d))

    with es:
        X = sb("X", [128, DC, TPC])
        rt = sb("rt", [128, DC, NST], BF16)
        wdt = sb("wdt", [128, 2, CH], BF16)
        smf = sb("smf", [128, 80])
        bdt = smf[:, 0:2]
        nbdt = smf[:, 2:4]
        aneg = smf[:, 4:20].rearrange("p (c n) -> p c n", c=2)
        dpar = smf[:, 20:36]
        stt = smf[:, 36:52]
        hb = smf[:, 52:68]
        ones_c = sb("ones_c", [128, 128], BF16)
        onesf = sb("onesf", [128, 128])
        lnt = {}

        ps8 = es.enter_context(nc.psum_tensor("ps8", [128, 8, 512], F32))

        OWN = {}

        class Ctr:
            def __init__(self, name):
                self.sem = es.enter_context(nc.semaphore(name))
                self.val = 0
        bs = {e: Ctr(f"bar_{e}") for e in ENG}
        dq = Ctr("dq"); dg = Ctr("dg"); ds = Ctr("ds")
        smm = Ctr("smm"); svv = Ctr("svv"); sss = Ctr("sss"); sgg = Ctr("sgg")
        scc = Ctr("scc")
        DQ = {"q": dq, "g": dg, "s": ds}
        OWN.update({"t": smm, "v": svv, "s": sss, "g": sgg})

        def bar():
            for e in ENG:
                bs[e].val += 1
            for e, eng in ENG.items():
                if e in DQ:
                    eng.wait_ge(DQ[e].sem, DQ[e].val)
                if e in OWN:
                    eng.wait_ge(OWN[e].sem, OWN[e].val)
                eng.sem_inc(bs[e].sem, 1)
            for e, eng in ENG.items():
                for o in ENG:
                    if o != e:
                        eng.wait_ge(bs[o].sem, bs[o].val)

        def dma(k, out, in_):
            c = DQ[k]
            if k == "g":
                ENG[k].wait_ge(c.sem, c.val)
            c.val += 16
            ENG[k].dma_start(out=out, in_=in_).then_inc(c.sem, 16)

        def mark(ctr, eng):
            eng.nop().then_inc(ctr.sem, 1)
            ctr.val += 1

        def imark(ctr, inst):
            inst.then_inc(ctr.sem, 1)
            ctr.val += 1

        def schain(ctr, eng, inst):
            imark(ctr, inst)
            eng.wait_ge(ctr.sem, ctr.val)

        with nc.Block():
            dma("g", rt[:], rt_d.rearrange("c p e -> p c e"))
            dma("g", wdt[:], wdt_d.rearrange("c p m -> p c m"))
            dma("q", hb[:], bb_d)
            dma("g", ones_c[:], ones_col_d)
            dma("q", onesf[:], ones_col_d)
            dma("q", bdt[:], bdt_d)
            dma("q", dpar[:], dpar_d)
            dma("q", aneg[:], alog_d)
            S.wait_ge(dq.sem, dq.val)
            imark(sss, S.activation(hb[:], hb[:], AF.Copy, scale=0.5))
            imark(sss, S.activation(aneg[:], aneg[:], AF.Exp))
            V.wait_ge(sss.sem, sss.val)
            V.tensor_scalar(aneg[:], aneg[:], -1.0, None, ALU.mult)
            imark(svv, V.tensor_scalar(nbdt[:], bdt[:], -1.0, None, ALU.mult))
            bar()

        def ln_stats(j, dst_bf, bk, mbc, rbc, dqv=None):
            sq_lt = dst_bf[:, :, j * TOK:(j + 1) * TOK]
            vtm = mbc
            psSx = ps8[:, bk, :]
            psSq = ps8[:, bk + 1, :]
            if dqv is not None:
                S.wait_ge(dq.sem, dqv)
                T.wait_ge(dq.sem, dqv)
            S.wait_ge(svv.sem, svv.val)
            imark(sss, S.activation(sq_lt, X[:, :, j * TOK:(j + 1) * TOK], AF.Square))
            T.wait_ge(svv.sem, svv.val)
            T.wait_ge(sss.sem, sss.val)
            for kt in range(DC):
                T.matmul(psSx, onesf[:], X[:, kt, j * TOK:(j + 1) * TOK],
                         start=(kt == 0), stop=(kt == DC - 1))
            for kt in range(DC):
                mi = T.matmul(psSq, ones_c[:], sq_lt[:, kt, :],
                              start=(kt == 0), stop=(kt == DC - 1))
            imark(smm, mi)
            S.wait_ge(smm.sem, smm.val)
            S.activation(vtm[:], psSx, AF.Square, scale=1.0 / D)
            imark(sss, S.activation(rbc[:], psSq, AF.Copy, scale=1.0 / D))
            V.wait_ge(sss.sem, sss.val)
            imark(svv, V.scalar_tensor_tensor(rbc[:], rbc[:], EPS, vtm[:],
                                              ALU.add, ALU.subtract))
            S.wait_ge(svv.sem, svv.val)
            imark(sss, S.activation(mbc[:], psSx, AF.Copy, scale=1.0 / D))
            T.wait_ge(sss.sem, sss.val)
            schain(sss, S, S.activation(rbc[:], rbc[:], AF.Ln))
            imark(sss, S.activation(rbc[:], rbc[:], AF.Exp, scale=-0.5))

        def ln_apply(j, dst_bf, mbc, rbc, extra_ap=None, post_dc=None):
            scr = lnt["scr"]
            subm = {}
            usem = {}
            V.wait_ge(sss.sem, sss.val)
            V.wait_ge(svv.sem, svv.val)
            imark(svv, V.tensor_tensor(scr[0][:], X[:, 0, j * TOK:(j + 1) * TOK],
                                       mbc[:], ALU.subtract))
            subm[0] = svv.val
            for dc in range(DC):
                if dc + 1 < DC:
                    if dc >= 1:
                        V.wait_ge(svv.sem, usem[dc - 1])
                    imark(svv, V.tensor_tensor(scr[(dc + 1) % 2][:],
                                               X[:, dc + 1, j * TOK:(j + 1) * TOK],
                                               mbc[:], ALU.subtract))
                    subm[dc + 1] = svv.val
                V.wait_ge(svv.sem, subm[dc])
                mi = V.tensor_tensor(dst_bf[:, dc, j * TOK:(j + 1) * TOK],
                                     scr[dc % 2][:], rbc[:], ALU.mult)
                if extra_ap is not None:
                    mi = V.tensor_tensor(extra_ap(dc), scr[dc % 2][:], rbc[:],
                                         ALU.mult)
                imark(svv, mi)
                usem[dc] = svv.val
                if post_dc is not None:
                    post_dc(dc)
            S.wait_ge(svv.sem, svv.val)

        def ffn_sublayer(tn, pre_tile=None, after_tile=None):
            xlnb, m8st = tn["xlnb"], tn["m8st"]
            logsb = tn["logsb"]
            mtok, mbuf, hmk, mx8 = tn["mtok"], tn["mbuf"], tn["hmk"], tn["mx8"]
            wst, wb2 = tn["wst"], tn["wb2"]
            xt = tn.get("xt")
            psLog = ps8[:, 4, 0:NST]
            psT8 = ps8[0:NST, 5, 0:64].bitcast(BF16)
            psT8w = ps8[:, 5, 0:64]
            psH = ps8[:, 6, :]
            psM = ps8[:, 7, :]
            wb_src = wb_d.rearrange("c p o -> p c o")

            NU = 8
            ldm = {}
            cmark = {}
            qmm = {}
            st = {"nld": 0, "ncst": 0}

            def emit_load(k):
                if k >= 2:
                    Q.wait_ge(sss.sem, cmark[k - 2])
                qq = (k // 2) % 4
                dma("q", wst[:, :, (k % 2) * 256:(k % 2) * 256 + 256],
                    wb_src[:, :, qq * 512 + (k % 2) * 256:
                               qq * 512 + (k % 2) * 256 + 256])
                Q.wait_ge(dq.sem, dq.val)
                ldm[k] = dq.val

            def emit_cast(k):
                u, hh = divmod(k, 2)
                S.wait_ge(dq.sem, ldm[k])
                if u >= 2 and hh == 0:
                    S.wait_ge(smm.sem, qmm[u - 2])
                imark(sss, S.activation(wb2[u % 2][:, :, hh * 256:(hh + 1) * 256],
                                        wst[:, :, (k % 2) * 256:(k % 2) * 256 + 256],
                                        AF.Copy, scale=0.5))
                cmark[k] = sss.val

            def pump(n):
                n = min(n, 2 * NU)
                while st["ncst"] < n:
                    while st["nld"] < min(st["ncst"] + 2, 2 * NU):
                        emit_load(st["nld"]); st["nld"] += 1
                    emit_cast(st["ncst"]); st["ncst"] += 1
                while st["nld"] < min(n + 2, 2 * NU):
                    emit_load(st["nld"]); st["nld"] += 1

            if tn.get("war_svv") is not None:
                Q.wait_ge(svv.sem, tn["war_svv"])
            pump(0)

            ep_marks = []
            g_idx = 0
            for j in range(LT):
                if pre_tile is not None:
                    pre_tile(j)
                mbc, rbc = lnt[f"mbc{j}"], lnt[f"rbc{j}"]
                ln_stats(j, xlnb, 4, mbc, rbc, dqv=(xt[j] if xt else None))
                ln_apply(j, xlnb, mbc, rbc)
                am = svv.val
                T.wait_ge(svv.sem, am)
                T.wait_ge(sss.sem, sss.val)
                for half in range(4):
                    t0 = j * TOK + half * 128
                    for kt in range(DC):
                        mi = T.matmul(psLog, xlnb[:, kt, t0:t0 + 128],
                                      rt[:, kt, :], start=(kt == 0),
                                      stop=(kt == DC - 1))
                    imark(smm, mi)
                    S.wait_ge(smm.sem, smm.val)
                    S.wait_ge(svv.sem, svv.val)
                    imark(sss, S.copy(logsb[:], psLog))
                    T.wait_ge(sss.sem, sss.val)
                    V.wait_ge(sss.sem, sss.val)
                    schain(svv, V, V.reduce_max(mx8[:], logsb[:], axis=AX.X))
                    imark(svv, V.tensor_scalar(mtok[:, j, half, :], logsb[:],
                                               mx8[:], 1.0,
                                               ALU.is_equal, ALU.mult))
                if j == 0 and tn.get("gv") is not None:
                    T.wait_ge(dg.sem, tn["gv"])
                T.wait_ge(svv.sem, svv.val)
                for half in range(4):
                    imark(smm, T.matmul(psT8, mtok[:, j, half, :], tn["eye"][:],
                                        is_transpose=True, start=True, stop=True))
                    S.wait_ge(smm.sem, smm.val)
                    imark(sss, S.copy(m8st[:, half, :], psT8w))
                    T.wait_ge(sss.sem, sss.val)
                imark(smm, T.matmul(psM, tn["e16"][:],
                      m8st[0:NST, :, :].bitcast(BF16).rearrange("p h t -> p (h t)"),
                      start=True, stop=True))
                S.wait_ge(smm.sem, smm.val)
                S.wait_ge(svv.sem, svv.val)
                imark(sss, S.copy(mbuf[:], psM))
                T.wait_ge(sss.sem, sss.val)
                for kt in range(DC):
                    mi = T.matmul(psH, tn["at"][:, kt, :],
                                  xlnb[:, kt, j * TOK:(j + 1) * TOK],
                                  start=(kt == 0), stop=(kt == DC - 1))
                imark(smm, mi)
                V.wait_ge(smm.sem, smm.val)
                V.wait_ge(sss.sem, sss.val)
                imark(svv, V.tensor_tensor(hmk[:, j, :], psH, mbuf[:], ALU.mult))
                T.wait_ge(svv.sem, svv.val)
                for h in range(4):
                    u = j * 4 + h
                    pump(2 * u + 2)
                    wcur = wb2[u % 2]
                    T.wait_ge(sss.sem, cmark[2 * u + 1])
                    b0 = (g_idx % 2) * 4
                    if g_idx >= 2:
                        T.wait_ge(svv.sem, ep_marks[g_idx - 2])
                    for i in range(4):
                        og = h * 4 + i
                        psO = ps8[:, b0 + i, :]
                        for kt in range(DC):
                            T.matmul(psO, wcur[:, kt, i * 128:(i + 1) * 128],
                                     xlnb[:, kt, j * TOK:(j + 1) * TOK],
                                     start=(kt == 0), stop=False)
                        mi = T.matmul(psO, tn["bw"][:, og * 128:(og + 1) * 128],
                                      hmk[:, j, :], start=False, stop=True)
                    imark(smm, mi)
                    qmm[u] = smm.val
                    V.wait_ge(smm.sem, smm.val)
                    for i in range(4):
                        og = h * 4 + i
                        mi = V.scalar_tensor_tensor(
                            X[:, og, j * TOK:(j + 1) * TOK],
                            ps8[:, b0 + i, :], hb[:, og:og + 1],
                            X[:, og, j * TOK:(j + 1) * TOK],
                            ALU.add, ALU.add)
                    imark(svv, mi)
                    ep_marks.append(svv.val)
                    g_idx += 1
                    pump(2 * (u + 3))
                if after_tile is not None:
                    after_tile(j)
            bar()

        cc_mark = {}
        cc_mark = {}
        a2_mark = {}

        es1 = ExitStack()
        sb1 = lambda n, s, d=F32: es1.enter_context(nc.sbuf_tensor(n, s, d))
        with es1:
            tn1 = dict(
                wbh=sb1("wbh", [128, DC, 512], BF16),
                wb2=None,
                xlnb=sb1("xlnb", [128, DC, TPC], BF16),
                mbuf=sb1("mbuf", [128, TOK], BF16),
                hmk=sb1("hmk", [128, LT, TOK], BF16),
            )
            smp1 = sb1("smp1", [128, 272])
            tn1["m8st"] = smp1[:, 16:272].rearrange("p (h t) -> p h t", h=4)
            tn1["mx8"] = smp1[:, 8:9]
            tn1["logsb"] = smp1[:, 0:8]
            smb1 = sb1("smb1", [128, 192], BF16)
            tn1["mtok"] = smb1[:, 0:64].rearrange("p (j h n) -> p j h n", j=LT, h=4)
            tn1["eye"] = smb1[:, 64:192]
            tn1["wbhB"] = sb1("wbhB", [128, DC, 512], BF16)
            tn1["wb2"] = [tn1["wbh"], tn1["wbhB"]]
            wst1 = sb1("wst1", [128, DC, 512])
            tn1["wst"] = wst1
            xl2 = sb1("xl2", [128, 4, TOK])
            wx = sb1("wx", [128, DC, 384], BF16)
            tn1["at"] = sb1("at", [128, DC, 128], BF16)
            tn1["bw"] = sb1("bw", [128, D], BF16)
            tn1["e16"] = sb1("e16_s", [NST, 128], BF16)
            for nm in ("mbc0", "rbc0", "mbc1", "rbc1"):
                lnt[nm] = sb1(nm, [128, TOK])
            lnt["scr"] = [sb1(f"scr{p}", [128, TOK]) for p in range(2)]

            with nc.Block():
                xt = []
                for j in range(LT):
                    dma("q", X[:, :, j * TOK:(j + 1) * TOK],
                        x0_d[j].rearrange("c p t -> p c t"))
                    Q.wait_ge(dq.sem, dq.val)
                    xt.append(dq.val)
                tn1["xt"] = xt
                dma("g", tn1["eye"][:], eye_d)
                dma("g", tn1["e16"][:], e16_d)
                dma("g", tn1["at"][:], at_d.rearrange("c p e -> p c e"))
                dma("g", tn1["bw"][:], bw_d)
                tn1["gv"] = dg.val
                dma("g", wx[:], wx_d.rearrange("c p k -> p c k"))
                wxv = dg.val
                stg_marks = []
                cc_mark.clear()

                def ln2_tile(j):
                    b = j
                    pm = {}

                    def eslot(dc):
                        if stg_marks:
                            V.wait_ge(dq.sem, stg_marks[-1])
                        return xl2[:, dc % 4, :]

                    def dterm(dc, b=b, j=j):
                        V.wait_ge(svv.sem, pm[dc])
                        imark(svv, V.scalar_tensor_tensor(
                            X[:, dc, j * TOK:(j + 1) * TOK],
                            xl2[:, dc % 4, :], dpar[:, dc:dc + 1],
                            X[:, dc, j * TOK:(j + 1) * TOK],
                            ALU.mult, ALU.add))
                        if dc % 2 == 1:
                            p0 = (dc - 1) % 4
                            Q.wait_ge(svv.sem, svv.val)
                            dma("q", a1_in[b][dc // 2, :, :]
                                .rearrange("(c p) t -> p c t", p=128),
                                xl2[:, p0:p0 + 2, :])
                            Q.wait_ge(dq.sem, dq.val)
                            stg_marks.append(dq.val)

                    def post(dc):
                        pm[dc] = svv.val
                        if dc >= 1:
                            dterm(dc - 1)

                    ln_stats(j, tn1["xlnb"], 4, lnt[f"mbc{j}"], lnt[f"rbc{j}"])
                    ln_apply(j, tn1["xlnb"], lnt[f"mbc{j}"], lnt[f"rbc{j}"],
                             extra_ap=eslot, post_dc=post)
                    dterm(DC - 1)
                    T.wait_ge(svv.sem, svv.val)
                    T.wait_ge(sss.sem, sss.val)
                    T.wait_ge(dg.sem, wxv)
                    for ot in (0, 2, 1):
                        psXo = ps8[:, 4 + ot, :]
                        for kt in range(DC):
                            mi = T.matmul(psXo, wx[:, kt, ot * 128:(ot + 1) * 128],
                                          tn1["xlnb"][:, kt, j * TOK:(j + 1) * TOK],
                                          start=(kt == 0), stop=(kt == DC - 1))
                    imark(smm, mi)
                    xdbe = tn1["xlnb"][:, 0:3, j * TOK:(j + 1) * TOK]
                    S.wait_ge(smm.sem, smm.val)
                    imark(sss, S.copy(xdbe, ps8[:, 4:7, :]))
                    T.wait_ge(sss.sem, sss.val)
                    Q.wait_ge(sss.sem, sss.val)
                    dma("q", ag_in[b][0:DTR, :].rearrange("(c p) t -> p c t", p=128),
                        xdbe[:, 0:2, :])
                    dma("q", ag_in[b][DTR:HB, :], xdbe[0:16, 2, :])
                    Q.wait_ge(dq.sem, dq.val)
                    V.wait_ge(dq.sem, dq.val)
                    S.wait_ge(dq.sem, dq.val)
                    if TRUNC >= 4:
                        G.wait_ge(dq.sem, dq.val)
                        imark(scc, G.collective_compute(
                            "AllGather", ALU.bypass,
                            ins=[ag_inT[b]], outs=[ag_outT[b]],
                            replica_groups=[list(range(NC))]))
                        imark(scc, G.collective_compute(
                            "AllToAll", ALU.bypass,
                            ins=[a1_inT[b]], outs=[a1_outT[b]],
                            replica_groups=[list(range(NC))]))
                        cc_mark[b] = scc.val

                if TRUNC >= 2:
                    ffn_sublayer(tn1,
                                 after_tile=(ln2_tile if TRUNC >= 3 else None))

        es2 = ExitStack()
        sb2 = lambda n, s, d=F32: es2.enter_context(nc.sbuf_tensor(n, s, d))
        with es2:
            x2ch = [sb2(f"x2ch{p}", [128, 2, QT]) for p in range(2)]
            dtc = sb2("dtc", [128, 2, QT], BF16)
            dlt = [sb2(f"dlt{p}", [128, 2, QT]) for p in range(2)]
            dab = [sb2(f"dab{p}", [128, QT]) for p in range(3)]
            dbu = [sb2(f"dbu{p}", [128, QT]) for p in range(2)]
            hbt = [sb2(f"hbt{p}", [128, QT]) for p in range(2)]
            hp = [sb2(f"hp{p}", [128, QT]) for p in range(2)]
            ysc = [sb2(f"ysc{p}", [128, QT]) for p in range(2)]
            bq = [sb2(f"bq{p}", [128, NST, QT], BF16) for p in range(2)]
            cq = sb2("cq", [128, NST, QT], BF16)

            with nc.Block():
                bar()

                NCH = 8 if TRUNC >= 5 else 0
                ld_mark = {}
                cq_mark = {}
                mm_mark = {}
                sp_mark = {}
                st_mark = {}
                um_mark = {}
                vchunk = {}
                scan_marks = []
                hc_marks = []
                dbu_marks = []
                hp_marks = []

                def issue_loads(ci):
                    b, q4 = divmod(ci, 4)
                    par = ci % 2
                    c0 = 2 * q4
                    if TRUNC >= 4 and (ci == 0 or ci == 4):
                        Q.wait_ge(scc.sem, cc_mark[b])
                        S.wait_ge(scc.sem, cc_mark[b])
                    if ci >= 2 and (ci - 2) in vchunk:
                        Q.wait_ge(svv.sem, vchunk[ci - 2])
                    if ci >= 1:
                        S.wait_ge(svv.sem, um_mark[ci - 1])
                        Q.wait_ge(smm.sem, mm_mark[ci - 1])
                    for cx in range(2):
                        c = c0 + cx
                        dma("q", x2ch[par][:, :, cx * 512:(cx + 1) * 512],
                            a1_out[b][c, :, :].rearrange("(c2 p) t -> p c2 t", p=128))
                        dma("q", dtc[:, :, cx * 512:(cx + 1) * 512],
                            ag_out[b][c, 0:DTR, :].rearrange("(c2 p) t -> p c2 t", p=128))
                    Q.wait_ge(dq.sem, dq.val)
                    for n in range(NST):
                        dma("s", bq[par][:, n, :].rearrange("p (c t) -> p c t", c=2),
                            ag_out[b][c0:c0 + 2, DTR + n, :].partition_broadcast(128))
                    ld_mark[ci] = (dq.val, ds.val)

                def issue_cq(ci):
                    b, q4 = divmod(ci, 4)
                    c0 = 2 * q4
                    for n in range(NST):
                        dma("q", cq[:, n, :].rearrange("p (c t) -> p c t", c=2),
                            ag_out[b][c0:c0 + 2, DTR + NST + n, :].partition_broadcast(128))
                    Q.wait_ge(dq.sem, dq.val)
                    cq_mark[ci] = dq.val

                def issue_delta_mm(ci):
                    par = ci % 2
                    T.wait_ge(dq.sem, ld_mark[ci][0])
                    if ci >= 2:
                        T.wait_ge(svv.sem, um_mark[ci - 2])
                        T.wait_ge(sss.sem, sp_mark[ci - 2])
                    for ot in range(2):
                        for ch2 in range(2):
                            psDo = ps8[:, par * 4 + ot * 2 + ch2, :]
                            for kt in range(2):
                                mi = T.matmul(psDo, wdt[:, kt, ot * 128:(ot + 1) * 128],
                                              dtc[:, kt, ch2 * 512:(ch2 + 1) * 512],
                                              start=(kt == 0), stop=(kt == 1))
                    imark(smm, mi)
                    mm_mark[ci] = smm.val

                def issue_softplus(ci):
                    par = ci % 2
                    psD = ps8[:, par * 4:par * 4 + 4, :].rearrange("p (o c) t -> p o c t", o=2)
                    S.wait_ge(smm.sem, mm_mark[ci])
                    if ci >= 2:
                        S.wait_ge(svv.sem, vchunk.get(ci - 2, 0))
                    for ot in range(2):
                        mi = S.activation(dlt[par][:, ot, :], psD[:, ot, :, :], AF.Exp,
                                          scale=-1.0, bias=nbdt[:, ot:ot + 1])
                    schain(sss, S, mi)
                    imark(sss, S.activation(dlt[par][:], dlt[par][:], AF.Ln, bias=1.0))
                    sp_mark[ci] = sss.val

                issue_loads(0)
                issue_cq(0)
                issue_delta_mm(0)
                issue_softplus(0)

                it = 0
                for ci in range(NCH):
                    b, q4 = divmod(ci, 4)
                    par = ci % 2
                    psD = ps8[:, par * 4:par * 4 + 4, :].rearrange("p (o c) t -> p o c t", o=2)
                    V.wait_ge(sss.sem, sp_mark[ci])
                    V.wait_ge(dq.sem, ld_mark[ci][0])
                    V.wait_ge(ds.sem, ld_mark[ci][1])
                    for ot in range(2):
                        mi = V.scalar_tensor_tensor(dlt[par][:, ot, :], psD[:, ot, :, :],
                                                    bdt[:, ot:ot + 1], dlt[par][:, ot, :],
                                                    ALU.add, ALU.add)
                    schain(svv, V, mi)
                    schain(svv, V, V.tensor_tensor(x2ch[par][:], dlt[par][:],
                                                   x2ch[par][:], ALU.mult))
                    um_mark[ci] = svv.val
                    imark(svv, V.tensor_tensor(dbu[it % 2][:], x2ch[par][:, 0, :],
                                               bq[par][:, 0, :], ALU.mult))
                    dbu_marks.append(svv.val)
                    if ci + 1 < NCH:
                        issue_loads(ci + 1)
                        issue_delta_mm(ci + 1)
                    for dc in range(2):
                        acc_marks = []
                        for n in range(NST):
                            k = it
                            if dc == 0 and n == 0:
                                S.wait_ge(svv.sem, um_mark[ci])
                            if k >= 3:
                                S.wait_ge(svv.sem, scan_marks[k - 3])
                            imark(sss, S.activation(dab[k % 3][:], dlt[par][:, dc, :],
                                                    AF.Exp, scale=aneg[:, dc, n:n + 1]))
                            V.wait_ge(sss.sem, sss.val)
                            V.wait_ge(svv.sem, dbu_marks[k])
                            if q4 == 0:
                                mi = V.tensor_tensor_scan(hbt[k % 2][:], dab[k % 3][:],
                                                          dbu[k % 2][:], 0.0,
                                                          ALU.mult, ALU.add)
                            else:
                                mi = V.tensor_tensor_scan(hbt[k % 2][:], dab[k % 3][:],
                                                          dbu[k % 2][:],
                                                          stt[:, dc * 8 + n:dc * 8 + n + 1],
                                                          ALU.mult, ALU.add)
                            imark(svv, mi)
                            scan_marks.append(svv.val)
                            if n < NST - 1 or dc == 0:
                                dcn, nn = (dc, n + 1) if n < NST - 1 else (1, 0)
                                if k >= 1:
                                    V.wait_ge(svv.sem, scan_marks[k - 1])
                                mi = V.tensor_tensor(dbu[(k + 1) % 2][:],
                                                     x2ch[par][:, dcn, :],
                                                     bq[par][:, nn, :], ALU.mult)
                                imark(svv, mi)
                                dbu_marks.append(svv.val)
                            if n == 0 and dc == 0:
                                V.wait_ge(dq.sem, cq_mark[ci])
                            V.wait_ge(svv.sem, scan_marks[k])
                            imark(svv, V.tensor_tensor(hp[k % 2][:], hbt[k % 2][:],
                                                       cq[:, n, :], ALU.mult))
                            hp_marks.append(svv.val)
                            if n >= 1:
                                V.wait_ge(svv.sem,
                                          acc_marks[-1] if acc_marks else hp_marks[k - 1])
                                if n == 1:
                                    if (ci - 1, dc) in st_mark:
                                        V.wait_ge(dq.sem, st_mark[(ci - 1, dc)])
                                    mi = V.tensor_copy(ysc[dc][:], hp[(k - 1) % 2][:])
                                else:
                                    mi = V.tensor_tensor(ysc[dc][:], ysc[dc][:],
                                                         hp[(k - 1) % 2][:], ALU.add)
                                imark(svv, mi)
                                acc_marks.append(svv.val)
                            if q4 != 3:
                                imark(svv, V.tensor_copy(
                                    stt[:, dc * 8 + n:dc * 8 + n + 1],
                                    hbt[k % 2][:, QT - 1:QT]))
                            it += 1
                        if dc == 0 and ci + 1 < NCH:
                            issue_softplus(ci + 1)
                        V.wait_ge(svv.sem, acc_marks[-1])
                        imark(svv, V.tensor_tensor(ysc[dc][:], ysc[dc][:],
                                                   hp[(it - 1) % 2][:], ALU.add))
                        c0 = 2 * q4
                        Q.wait_ge(svv.sem, svv.val)
                        for cx in range(2):
                            c = c0 + cx
                            dma("q", a2_in[b][c, dc * 128:(dc + 1) * 128, :],
                                ysc[dc][:, cx * 512:(cx + 1) * 512])
                        Q.wait_ge(dq.sem, dq.val)
                        st_mark[(ci, dc)] = dq.val
                        if TRUNC >= 6 and q4 == 3 and dc == 1:
                            G.wait_ge(dq.sem, dq.val)
                            imark(scc, G.collective_compute(
                                "AllToAll", ALU.bypass,
                                ins=[a2_inT[b]], outs=[a2_outT[b]],
                                replica_groups=[list(range(NC))]))
                            a2_mark[b] = scc.val
                    vchunk[ci] = svv.val
                    if ci + 1 < NCH:
                        Q.wait_ge(svv.sem, vchunk[ci])
                        issue_cq(ci + 1)
                bar()

        es3 = ExitStack()
        sb3 = lambda n, s, d=F32: es3.enter_context(nc.sbuf_tensor(n, s, d))
        with es3:
            tn3 = dict(
                wbh=sb3("wbh3", [128, DC, 512], BF16),
                wb2=None,
                xlnb=sb3("xlnb3", [128, DC, TPC], BF16),
                mbuf=sb3("mbuf3", [128, TOK], BF16),
                hmk=sb3("hmk3", [128, LT, TOK], BF16),
            )
            smp3 = sb3("smp3", [128, 272])
            tn3["m8st"] = smp3[:, 16:272].rearrange("p (h t) -> p h t", h=4)
            tn3["mx8"] = smp3[:, 8:9]
            tn3["logsb"] = smp3[:, 0:8]
            smb3 = sb3("smb3", [128, 192], BF16)
            tn3["mtok"] = smb3[:, 0:64].rearrange("p (j h n) -> p j h n", j=LT, h=4)
            tn3["eye"] = smb3[:, 64:192]
            tn3["wbhB"] = sb3("wbhB3", [128, DC, 512], BF16)
            tn3["wb2"] = [tn3["wbh"], tn3["wbhB"]]
            wst3 = sb3("wst3", [128, DC, 512])
            tn3["wst"] = wst3
            ytl = (tn3["xlnb"].rearrange("p c t -> p (c t)").bitcast(F32)
                   .rearrange("p (c t) -> p c t", c=DC))
            tn3["at"] = sb3("at3", [128, DC, 128], BF16)
            tn3["bw"] = sb3("bw3", [128, D], BF16)
            tn3["e16"] = sb3("e16_s3", [NST, 128], BF16)
            for nm in ("mbc0", "rbc0", "mbc1", "rbc1"):
                lnt[nm] = sb3(nm + "3", [128, TOK])
            lnt["scr"] = [sb3(f"scr{p}3", [128, TOK]) for p in range(2)]

            with nc.Block():
                dma("g", tn3["eye"][:], eye_d)
                dma("g", tn3["e16"][:], e16_d)
                dma("g", tn3["at"][:], at_d.rearrange("c p e -> p c e"))
                dma("g", tn3["bw"][:], bw_d)
                tn3["gv"] = dg.val

                def resid(j):
                    b = j
                    S.wait_ge(scc.sem, a2_mark[b])
                    S.wait_ge(svv.sem, svv.val)
                    S.wait_ge(smm.sem, smm.val)
                    for g8 in range(NC):
                        dma("s", ytl[:, 2 * g8:2 * g8 + 2, :],
                            a2_out[b][g8, :, :].rearrange("(c2 p) t -> p c2 t",
                                                          p=128))
                    V.wait_ge(ds.sem, ds.val)
                    imark(svv, V.tensor_tensor(X[:, :, j * TOK:(j + 1) * TOK],
                                               X[:, :, j * TOK:(j + 1) * TOK],
                                               ytl[:], ALU.add))

                def pre3(j):
                    if j == 1 and TRUNC >= 7:
                        resid(1)

                def store_tile(j):
                    S.wait_ge(svv.sem, svv.val)
                    dma("s", out_d[j].rearrange("c p t -> p c t"),
                        X[:, :, j * TOK:(j + 1) * TOK])

                if TRUNC >= 7:
                    resid(0)
                if TRUNC >= 8:
                    ffn_sublayer(tn3, pre_tile=pre3, after_tile=store_tile)
                else:
                    dma("q", out_d.rearrange("l c p t -> p l c t"),
                        X[:].rearrange("p c (l t) -> p l c t", l=LT))
                for eng in ENG.values():
                    eng.wait_ge(dq.sem, dq.val)
                    eng.wait_ge(ds.sem, ds.val)

    return nc


_NC_CACHE = {}


def kernel(L0_attn_x, L0_attn_W_xproj, L0_attn_W_dt, L0_attn_b_dt, L0_attn_A_log,
           L0_attn_D_param, L0_ffn_W_base, L0_ffn_b_base, L0_ffn_router_W,
           L0_ffn_A, L0_ffn_Bw):
    x = np.ascontiguousarray(np.asarray(L0_attn_x, dtype=np.float32).reshape(NC * TPC, D).T)
    wbT = np.ascontiguousarray(np.asarray(L0_ffn_W_base, np.float32).T).reshape(DC, 128, D)
    bb = np.ascontiguousarray(np.asarray(L0_ffn_b_base, np.float32).reshape(DC, 128).T)
    rtT = np.ascontiguousarray(np.asarray(L0_ffn_router_W, np.float32).T).reshape(DC, 128, NST)
    atT = np.ascontiguousarray(np.asarray(L0_ffn_A, np.float32).reshape(128, D).T).reshape(DC, 128, 128)
    bwT = np.ascontiguousarray(np.transpose(np.asarray(L0_ffn_Bw, np.float32), (0, 2, 1)).reshape(128, D))
    wxT = np.zeros((D, 384), np.float32)
    wxT[:, 0:272] = np.asarray(L0_attn_W_xproj, np.float32).T
    wxT = wxT.reshape(DC, 128, 384)
    ones_col = np.ones((128, 128), np.float32)
    ones_row = np.ones((1, TOK), np.float32)
    eye = np.eye(128, dtype=np.float32)
    e16 = np.zeros((NST, 128), np.float32)
    for e in range(NST):
        e16[e, e * 16:(e + 1) * 16] = 1.0
    dparv = np.ascontiguousarray(np.asarray(L0_attn_D_param, np.float32).reshape(DC, 128).T)
    W_dt = np.asarray(L0_attn_W_dt, np.float32)
    b_dt = np.asarray(L0_attn_b_dt, np.float32)
    A_log = np.asarray(L0_attn_A_log, np.float32)

    key = os.environ.get("KTRUNC", "9")
    if key not in _NC_CACHE:
        _NC_CACHE[key] = build_program()
    nc = _NC_CACHE[key]

    in_maps = []
    for c in range(NC):
        cols = np.concatenate([np.arange((8 * j + c) * TOK, (8 * j + c + 1) * TOK)
                               for j in range(LT)])
        x0c = np.ascontiguousarray(
            x[:, cols].reshape(D, LT, TOK).transpose(1, 0, 2)).reshape(
            LT, DC, 128, TOK)
        wdtT = np.ascontiguousarray(W_dt[CH * c:CH * (c + 1), :].T).reshape(2, 128, CH)
        in_maps.append(dict(
            x0=x0c, wbT=wbT, bb=bb, routerT=rtT, AT=atT, BwT=bwT, WxpT=wxT,
            WdtT=wdtT,
            bdt=np.ascontiguousarray(b_dt[CH * c:CH * (c + 1)].reshape(2, 128).T),
            Alog=np.ascontiguousarray(
                A_log[CH * c:CH * (c + 1), :].reshape(2, 128, NST).transpose(1, 0, 2)),
            Dpar=dparv, ones_col=ones_col, ones_row=ones_row, eye=eye, E16=e16,
        ))
    ncore = int(os.environ.get("KCORES", str(NC)))
    res = run_bass_kernel_spmd(nc, in_maps[:ncore], core_ids=list(range(ncore)))
    out = np.zeros((D, NC * TPC), np.float32)
    for c in range(ncore):
        oc = res.results[c]["out"].reshape(LT, D, TOK)
        for j in range(LT):
            t0 = (8 * j + c) * TOK
            out[:, t0:t0 + TOK] = oc[j]
    return np.ascontiguousarray(out.T).reshape(2, 4096, D)


# version 19
# speedup vs baseline: 1.0402x; 1.0402x over previous
import os
os.environ.setdefault("TRN_ALLOW_ICI_ALL_TO_ALL", "1")

import numpy as np
from contextlib import ExitStack

import concourse.bass as bass
import concourse.mybir as mybir
from concourse.bass_utils import run_bass_kernel_spmd

F32 = mybir.dt.float32
BF16 = mybir.dt.bfloat16
AF = mybir.ActivationFunctionType
ALU = mybir.AluOpType
AX = mybir.AxisListType

NC = 8
D = 2048
DC = 16
TPC = 1024
LT = 2
TOK = 512
NB = 2
NST = 8
DTR = 256
CH = 256
HB = DTR + 16
QT = 1024
EPS = 1e-5


def build_program():
    TRUNC = int(os.environ.get("KTRUNC", "9"))
    nc = bass.Bass("TRN2", target_bir_lowering=False)

    def din(name, shape, dt=F32):
        return nc.dram_tensor(name, shape, dt, kind="ExternalInput").ap()

    x0_d = din("x0", [LT, DC, 128, TOK])
    wb_d = din("wbT", [DC, 128, D])
    bb_d = din("bb", [128, DC])
    rt_d = din("routerT", [DC, 128, NST])
    at_d = din("AT", [DC, 128, 128])
    bw_d = din("BwT", [128, D])
    wx_d = din("WxpT", [DC, 128, 384])
    wdt_d = din("WdtT", [2, 128, CH])
    bdt_d = din("bdt", [128, 2])
    alog_d = din("Alog", [128, 2, NST])
    dpar_d = din("Dpar", [128, DC])
    ones_col_d = din("ones_col", [128, 128])
    ones_row_d = din("ones_row", [1, TOK])
    eye_d = din("eye", [128, 128])
    e16_d = din("E16", [NST, 128])

    out_d = nc.dram_tensor("out", [LT, DC, 128, TOK], F32, kind="ExternalOutput").ap()

    ag_inT = nc.dram_tensor("ag_inT", [NB, HB, TOK], BF16, kind="Internal").ap()
    ag_outT = nc.dram_tensor("ag_outT", [NB, NC, HB, TOK], BF16, kind="Internal").ap()
    a1_inT = nc.dram_tensor("a1_inT", [NB, NC, CH, TOK], F32, kind="Internal").ap()
    a1_outT = nc.dram_tensor("a1_outT", [NB, NC, CH, TOK], F32, kind="Internal").ap()
    a2_inT = nc.dram_tensor("a2_inT", [NB, NC, CH, TOK], F32, kind="Internal").ap()
    a2_outT = nc.dram_tensor("a2_outT", [NB, NC, CH, TOK], F32, kind="Internal").ap()
    ag_in = [ag_inT[b] for b in range(NB)]
    ag_out = [ag_outT[b] for b in range(NB)]
    a1_in = [a1_inT[b] for b in range(NB)]
    a1_out = [a1_outT[b] for b in range(NB)]
    a2_in = [a2_inT[b] for b in range(NB)]
    a2_out = [a2_outT[b] for b in range(NB)]

    T, V, S, G, Q = nc.tensor, nc.vector, nc.scalar, nc.gpsimd, nc.sync
    ENG = {"t": T, "v": V, "s": S, "g": G, "q": Q}

    es = ExitStack()
    sb = lambda n, s, d=F32: es.enter_context(nc.sbuf_tensor(n, s, d))

    with es:
        X = sb("X", [128, DC, TPC])
        rt = sb("rt", [128, DC, NST], BF16)
        wdt = sb("wdt", [128, 2, CH], BF16)
        smf = sb("smf", [128, 80])
        bdt = smf[:, 0:2]
        nbdt = smf[:, 2:4]
        aneg = smf[:, 4:20].rearrange("p (c n) -> p c n", c=2)
        dpar = smf[:, 20:36]
        stt = smf[:, 36:52]
        hb = smf[:, 52:68]
        ones_c = sb("ones_c", [128, 128], BF16)
        onesf = sb("onesf", [128, 128])
        lnt = {}

        ps8 = es.enter_context(nc.psum_tensor("ps8", [128, 8, 512], F32))

        OWN = {}

        class Ctr:
            def __init__(self, name):
                self.sem = es.enter_context(nc.semaphore(name))
                self.val = 0
        bs = {e: Ctr(f"bar_{e}") for e in ENG}
        dq = Ctr("dq"); dg = Ctr("dg"); ds = Ctr("ds")
        smm = Ctr("smm"); svv = Ctr("svv"); sss = Ctr("sss"); sgg = Ctr("sgg")
        scc = Ctr("scc")
        DQ = {"q": dq, "g": dg, "s": ds}
        OWN.update({"t": smm, "v": svv, "s": sss, "g": sgg})

        def bar():
            for e in ENG:
                bs[e].val += 1
            for e, eng in ENG.items():
                if e in DQ:
                    eng.wait_ge(DQ[e].sem, DQ[e].val)
                if e in OWN:
                    eng.wait_ge(OWN[e].sem, OWN[e].val)
                eng.sem_inc(bs[e].sem, 1)
            for e, eng in ENG.items():
                for o in ENG:
                    if o != e:
                        eng.wait_ge(bs[o].sem, bs[o].val)

        def dma(k, out, in_):
            c = DQ[k]
            if k == "g":
                ENG[k].wait_ge(c.sem, c.val)
            c.val += 16
            ENG[k].dma_start(out=out, in_=in_).then_inc(c.sem, 16)

        def mark(ctr, eng):
            eng.nop().then_inc(ctr.sem, 1)
            ctr.val += 1

        def imark(ctr, inst):
            inst.then_inc(ctr.sem, 1)
            ctr.val += 1

        def schain(ctr, eng, inst):
            imark(ctr, inst)
            eng.wait_ge(ctr.sem, ctr.val)

        with nc.Block():
            dma("g", rt[:], rt_d.rearrange("c p e -> p c e"))
            dma("g", wdt[:], wdt_d.rearrange("c p m -> p c m"))
            dma("q", hb[:], bb_d)
            dma("g", ones_c[:], ones_col_d)
            dma("q", onesf[:], ones_col_d)
            dma("q", bdt[:], bdt_d)
            dma("q", dpar[:], dpar_d)
            dma("q", aneg[:], alog_d)
            S.wait_ge(dq.sem, dq.val)
            imark(sss, S.activation(hb[:], hb[:], AF.Copy, scale=0.5))
            imark(sss, S.activation(aneg[:], aneg[:], AF.Exp))
            V.wait_ge(sss.sem, sss.val)
            V.tensor_scalar(aneg[:], aneg[:], -1.0, None, ALU.mult)
            imark(svv, V.tensor_scalar(nbdt[:], bdt[:], -1.0, None, ALU.mult))
            bar()

        def ln_stats(j, dst_bf, bk, mbc, rbc, dqv=None):
            sq_lt = dst_bf[:, :, j * TOK:(j + 1) * TOK]
            vtm = mbc
            psSx = ps8[:, bk, :]
            psSq = ps8[:, bk + 1, :]
            if dqv is not None:
                S.wait_ge(dq.sem, dqv)
                T.wait_ge(dq.sem, dqv)
            S.wait_ge(svv.sem, svv.val)
            imark(sss, S.activation(sq_lt, X[:, :, j * TOK:(j + 1) * TOK], AF.Square))
            T.wait_ge(svv.sem, svv.val)
            T.wait_ge(sss.sem, sss.val)
            for kt in range(DC):
                T.matmul(psSx, onesf[:], X[:, kt, j * TOK:(j + 1) * TOK],
                         start=(kt == 0), stop=(kt == DC - 1))
            for kt in range(DC):
                mi = T.matmul(psSq, ones_c[:], sq_lt[:, kt, :],
                              start=(kt == 0), stop=(kt == DC - 1))
            imark(smm, mi)
            S.wait_ge(smm.sem, smm.val)
            S.activation(vtm[:], psSx, AF.Square, scale=1.0 / D)
            imark(sss, S.activation(rbc[:], psSq, AF.Copy, scale=1.0 / D))
            V.wait_ge(sss.sem, sss.val)
            imark(svv, V.scalar_tensor_tensor(rbc[:], rbc[:], EPS, vtm[:],
                                              ALU.add, ALU.subtract))
            S.wait_ge(svv.sem, svv.val)
            imark(sss, S.activation(mbc[:], psSx, AF.Copy, scale=1.0 / D))
            T.wait_ge(sss.sem, sss.val)
            schain(sss, S, S.activation(rbc[:], rbc[:], AF.Ln))
            imark(sss, S.activation(rbc[:], rbc[:], AF.Exp, scale=-0.5))

        def ln_apply(j, dst_bf, mbc, rbc, extra_ap=None, post_dc=None):
            scr = lnt["scr"]
            subm = {}
            usem = {}
            V.wait_ge(sss.sem, sss.val)
            V.wait_ge(svv.sem, svv.val)
            imark(svv, V.tensor_tensor(scr[0][:], X[:, 0, j * TOK:(j + 1) * TOK],
                                       mbc[:], ALU.subtract))
            subm[0] = svv.val
            for dc in range(DC):
                if dc + 1 < DC:
                    if dc >= 1:
                        V.wait_ge(svv.sem, usem[dc - 1])
                    imark(svv, V.tensor_tensor(scr[(dc + 1) % 2][:],
                                               X[:, dc + 1, j * TOK:(j + 1) * TOK],
                                               mbc[:], ALU.subtract))
                    subm[dc + 1] = svv.val
                V.wait_ge(svv.sem, subm[dc])
                mi = V.tensor_tensor(dst_bf[:, dc, j * TOK:(j + 1) * TOK],
                                     scr[dc % 2][:], rbc[:], ALU.mult)
                if extra_ap is not None:
                    mi = V.tensor_tensor(extra_ap(dc), scr[dc % 2][:], rbc[:],
                                         ALU.mult)
                imark(svv, mi)
                usem[dc] = svv.val
                if post_dc is not None:
                    post_dc(dc)
            S.wait_ge(svv.sem, svv.val)

        def ffn_sublayer(tn, pre_tile=None, after_tile=None):
            xlnb, m8st = tn["xlnb"], tn["m8st"]
            logsb = tn["logsb"]
            mtok, mbuf, hmk, mx8 = tn["mtok"], tn["mbuf"], tn["hmk"], tn["mx8"]
            wst, wb2 = tn["wst"], tn["wb2"]
            xt = tn.get("xt")
            psLog = ps8[:, 4, 0:NST]
            psT8 = ps8[0:NST, 5, 0:64].bitcast(BF16)
            psT8w = ps8[:, 5, 0:64]
            psH = ps8[:, 6, :]
            psM = ps8[:, 7, :]
            wb_src = wb_d.rearrange("c p o -> p c o")

            NU = 8
            ldm = {}
            cmark = {}
            qmm = {}
            st = {"nld": 0, "ncst": 0}

            def emit_load(k):
                if k >= 2:
                    Q.wait_ge(sss.sem, cmark[k - 2])
                qq = (k // 2) % 4
                dma("q", wst[:, :, (k % 2) * 256:(k % 2) * 256 + 256],
                    wb_src[:, :, qq * 512 + (k % 2) * 256:
                               qq * 512 + (k % 2) * 256 + 256])
                Q.wait_ge(dq.sem, dq.val)
                ldm[k] = dq.val

            def emit_cast(k):
                u, hh = divmod(k, 2)
                S.wait_ge(dq.sem, ldm[k])
                if u >= 2 and hh == 0:
                    S.wait_ge(smm.sem, qmm[u - 2])
                imark(sss, S.activation(wb2[u % 2][:, :, hh * 256:(hh + 1) * 256],
                                        wst[:, :, (k % 2) * 256:(k % 2) * 256 + 256],
                                        AF.Copy, scale=0.5))
                cmark[k] = sss.val

            def pump(n):
                n = min(n, 2 * NU)
                while st["ncst"] < n:
                    while st["nld"] < min(st["ncst"] + 2, 2 * NU):
                        emit_load(st["nld"]); st["nld"] += 1
                    emit_cast(st["ncst"]); st["ncst"] += 1
                while st["nld"] < min(n + 2, 2 * NU):
                    emit_load(st["nld"]); st["nld"] += 1

            if tn.get("war_svv") is not None:
                Q.wait_ge(svv.sem, tn["war_svv"])
            pump(0)

            ep_marks = []
            g_idx = 0
            for j in range(LT):
                if pre_tile is not None:
                    pre_tile(j)
                mbc, rbc = lnt[f"mbc{j}"], lnt[f"rbc{j}"]
                ln_stats(j, xlnb, 4, mbc, rbc, dqv=(xt[j] if xt else None))
                ln_apply(j, xlnb, mbc, rbc)
                am = svv.val
                T.wait_ge(svv.sem, am)
                T.wait_ge(sss.sem, sss.val)
                for half in range(4):
                    t0 = j * TOK + half * 128
                    for kt in range(DC):
                        mi = T.matmul(psLog, xlnb[:, kt, t0:t0 + 128],
                                      rt[:, kt, :], start=(kt == 0),
                                      stop=(kt == DC - 1))
                    imark(smm, mi)
                    S.wait_ge(smm.sem, smm.val)
                    S.wait_ge(svv.sem, svv.val)
                    imark(sss, S.copy(logsb[:], psLog))
                    T.wait_ge(sss.sem, sss.val)
                    V.wait_ge(sss.sem, sss.val)
                    schain(svv, V, V.reduce_max(mx8[:], logsb[:], axis=AX.X))
                    imark(svv, V.tensor_scalar(mtok[:, j, half, :], logsb[:],
                                               mx8[:], 1.0,
                                               ALU.is_equal, ALU.mult))
                if j == 0 and tn.get("gv") is not None:
                    T.wait_ge(dg.sem, tn["gv"])
                T.wait_ge(svv.sem, svv.val)
                for half in range(4):
                    imark(smm, T.matmul(psT8, mtok[:, j, half, :], tn["eye"][:],
                                        is_transpose=True, start=True, stop=True))
                    S.wait_ge(smm.sem, smm.val)
                    imark(sss, S.copy(m8st[:, half, :], psT8w))
                    T.wait_ge(sss.sem, sss.val)
                imark(smm, T.matmul(psM, tn["e16"][:],
                      m8st[0:NST, :, :].bitcast(BF16).rearrange("p h t -> p (h t)"),
                      start=True, stop=True))
                S.wait_ge(smm.sem, smm.val)
                S.wait_ge(svv.sem, svv.val)
                imark(sss, S.copy(mbuf[:], psM))
                T.wait_ge(sss.sem, sss.val)
                for kt in range(DC):
                    mi = T.matmul(psH, tn["at"][:, kt, :],
                                  xlnb[:, kt, j * TOK:(j + 1) * TOK],
                                  start=(kt == 0), stop=(kt == DC - 1))
                imark(smm, mi)
                V.wait_ge(smm.sem, smm.val)
                V.wait_ge(sss.sem, sss.val)
                imark(svv, V.tensor_tensor(hmk[:, j, :], psH, mbuf[:], ALU.mult))
                T.wait_ge(svv.sem, svv.val)
                for h in range(4):
                    u = j * 4 + h
                    pump(2 * u + 2)
                    wcur = wb2[u % 2]
                    T.wait_ge(sss.sem, cmark[2 * u + 1])
                    b0 = (g_idx % 2) * 4
                    if g_idx >= 2:
                        T.wait_ge(svv.sem, ep_marks[g_idx - 2])
                    for i in range(4):
                        og = h * 4 + i
                        psO = ps8[:, b0 + i, :]
                        for kt in range(DC):
                            T.matmul(psO, wcur[:, kt, i * 128:(i + 1) * 128],
                                     xlnb[:, kt, j * TOK:(j + 1) * TOK],
                                     start=(kt == 0), stop=False)
                        mi = T.matmul(psO, tn["bw"][:, og * 128:(og + 1) * 128],
                                      hmk[:, j, :], start=False, stop=True)
                    imark(smm, mi)
                    qmm[u] = smm.val
                    V.wait_ge(smm.sem, smm.val)
                    for i in range(4):
                        og = h * 4 + i
                        mi = V.scalar_tensor_tensor(
                            X[:, og, j * TOK:(j + 1) * TOK],
                            ps8[:, b0 + i, :], hb[:, og:og + 1],
                            X[:, og, j * TOK:(j + 1) * TOK],
                            ALU.add, ALU.add)
                    imark(svv, mi)
                    ep_marks.append(svv.val)
                    g_idx += 1
                    pump(2 * (u + 3))
                if after_tile is not None:
                    after_tile(j)
            bar()

        cc_mark = {}
        cc_mark = {}
        a2_mark = {}

        es1 = ExitStack()
        sb1 = lambda n, s, d=F32: es1.enter_context(nc.sbuf_tensor(n, s, d))
        with es1:
            tn1 = dict(
                wbh=sb1("wbh", [128, DC, 512], BF16),
                wb2=None,
                xlnb=sb1("xlnb", [128, DC, TPC], BF16),
                mbuf=sb1("mbuf", [128, TOK], BF16),
                hmk=sb1("hmk", [128, LT, TOK], BF16),
            )
            smp1 = sb1("smp1", [128, 272])
            tn1["m8st"] = smp1[:, 16:272].rearrange("p (h t) -> p h t", h=4)
            tn1["mx8"] = smp1[:, 8:9]
            tn1["logsb"] = smp1[:, 0:8]
            smb1 = sb1("smb1", [128, 192], BF16)
            tn1["mtok"] = smb1[:, 0:64].rearrange("p (j h n) -> p j h n", j=LT, h=4)
            tn1["eye"] = smb1[:, 64:192]
            tn1["wbhB"] = sb1("wbhB", [128, DC, 512], BF16)
            tn1["wb2"] = [tn1["wbh"], tn1["wbhB"]]
            wst1 = sb1("wst1", [128, DC, 512])
            tn1["wst"] = wst1
            xl2 = sb1("xl2", [128, 4, TOK])
            wx = sb1("wx", [128, DC, 384], BF16)
            tn1["at"] = sb1("at", [128, DC, 128], BF16)
            tn1["bw"] = sb1("bw", [128, D], BF16)
            tn1["e16"] = sb1("e16_s", [NST, 128], BF16)
            for nm in ("mbc0", "rbc0", "mbc1", "rbc1"):
                lnt[nm] = sb1(nm, [128, TOK])
            lnt["scr"] = [sb1(f"scr{p}", [128, TOK]) for p in range(2)]

            with nc.Block():
                xt = []
                for j in range(LT):
                    dma("q", X[:, :, j * TOK:(j + 1) * TOK],
                        x0_d[j].rearrange("c p t -> p c t"))
                    Q.wait_ge(dq.sem, dq.val)
                    xt.append(dq.val)
                tn1["xt"] = xt
                dma("g", tn1["eye"][:], eye_d)
                dma("g", tn1["e16"][:], e16_d)
                dma("g", tn1["at"][:], at_d.rearrange("c p e -> p c e"))
                dma("g", tn1["bw"][:], bw_d)
                tn1["gv"] = dg.val
                dma("g", wx[:], wx_d.rearrange("c p k -> p c k"))
                wxv = dg.val
                stg_marks = []
                cc_mark.clear()

                def ln2_tile(j):
                    b = j
                    pm = {}

                    def eslot(dc):
                        s = dc % 4
                        for mark, pair in reversed(stg_marks):
                            if s in pair:
                                V.wait_ge(dq.sem, mark)
                                break
                        return xl2[:, dc % 4, :]

                    def dterm(dc, b=b, j=j):
                        V.wait_ge(svv.sem, pm[dc])
                        imark(svv, V.scalar_tensor_tensor(
                            X[:, dc, j * TOK:(j + 1) * TOK],
                            xl2[:, dc % 4, :], dpar[:, dc:dc + 1],
                            X[:, dc, j * TOK:(j + 1) * TOK],
                            ALU.mult, ALU.add))
                        if dc % 2 == 1:
                            p0 = (dc - 1) % 4
                            Q.wait_ge(svv.sem, svv.val)
                            dma("q", a1_in[b][dc // 2, :, :]
                                .rearrange("(c p) t -> p c t", p=128),
                                xl2[:, p0:p0 + 2, :])
                            Q.wait_ge(dq.sem, dq.val)
                            stg_marks.append((dq.val, ((dc - 1) % 4, dc % 4)))

                    def post(dc):
                        pm[dc] = svv.val
                        if dc >= 1:
                            dterm(dc - 1)

                    ln_stats(j, tn1["xlnb"], 4, lnt[f"mbc{j}"], lnt[f"rbc{j}"])
                    ln_apply(j, tn1["xlnb"], lnt[f"mbc{j}"], lnt[f"rbc{j}"],
                             extra_ap=eslot, post_dc=post)
                    dterm(DC - 1)
                    T.wait_ge(svv.sem, svv.val)
                    T.wait_ge(sss.sem, sss.val)
                    T.wait_ge(dg.sem, wxv)
                    for ot in (0, 2, 1):
                        psXo = ps8[:, 4 + ot, :]
                        for kt in range(DC):
                            mi = T.matmul(psXo, wx[:, kt, ot * 128:(ot + 1) * 128],
                                          tn1["xlnb"][:, kt, j * TOK:(j + 1) * TOK],
                                          start=(kt == 0), stop=(kt == DC - 1))
                    imark(smm, mi)
                    xdbe = tn1["xlnb"][:, 0:3, j * TOK:(j + 1) * TOK]
                    S.wait_ge(smm.sem, smm.val)
                    imark(sss, S.copy(xdbe, ps8[:, 4:7, :]))
                    T.wait_ge(sss.sem, sss.val)
                    Q.wait_ge(sss.sem, sss.val)
                    dma("q", ag_in[b][0:DTR, :].rearrange("(c p) t -> p c t", p=128),
                        xdbe[:, 0:2, :])
                    dma("q", ag_in[b][DTR:HB, :], xdbe[0:16, 2, :])
                    Q.wait_ge(dq.sem, dq.val)
                    V.wait_ge(dq.sem, dq.val)
                    S.wait_ge(dq.sem, dq.val)
                    if TRUNC >= 4:
                        G.wait_ge(dq.sem, dq.val)
                        imark(scc, G.collective_compute(
                            "AllGather", ALU.bypass,
                            ins=[ag_inT[b]], outs=[ag_outT[b]],
                            replica_groups=[list(range(NC))]))
                        imark(scc, G.collective_compute(
                            "AllToAll", ALU.bypass,
                            ins=[a1_inT[b]], outs=[a1_outT[b]],
                            replica_groups=[list(range(NC))]))
                        cc_mark[b] = scc.val

                if TRUNC >= 2:
                    ffn_sublayer(tn1,
                                 after_tile=(ln2_tile if TRUNC >= 3 else None))

        es2 = ExitStack()
        sb2 = lambda n, s, d=F32: es2.enter_context(nc.sbuf_tensor(n, s, d))
        with es2:
            x2ch = [sb2(f"x2ch{p}", [128, 2, QT]) for p in range(2)]
            dtc = sb2("dtc", [128, 2, QT], BF16)
            dlt = [sb2(f"dlt{p}", [128, 2, QT]) for p in range(2)]
            dab = [sb2(f"dab{p}", [128, QT]) for p in range(3)]
            dbu = [sb2(f"dbu{p}", [128, QT]) for p in range(2)]
            hbt = [sb2(f"hbt{p}", [128, QT]) for p in range(2)]
            hp = [sb2(f"hp{p}", [128, QT]) for p in range(2)]
            ysc = [sb2(f"ysc{p}", [128, QT]) for p in range(2)]
            bq = [sb2(f"bq{p}", [128, NST, QT], BF16) for p in range(2)]
            cq = sb2("cq", [128, NST, QT], BF16)

            with nc.Block():
                bar()

                NCH = 8 if TRUNC >= 5 else 0
                ld_mark = {}
                cq_mark = {}
                mm_mark = {}
                sp_mark = {}
                st_mark = {}
                um_mark = {}
                vchunk = {}
                scan_marks = []
                hc_marks = []
                dbu_marks = []
                hp_marks = []

                def issue_loads(ci):
                    b, q4 = divmod(ci, 4)
                    par = ci % 2
                    c0 = 2 * q4
                    if TRUNC >= 4 and (ci == 0 or ci == 4):
                        Q.wait_ge(scc.sem, cc_mark[b])
                        S.wait_ge(scc.sem, cc_mark[b])
                    if ci >= 2 and (ci - 2) in vchunk:
                        Q.wait_ge(svv.sem, vchunk[ci - 2])
                    if ci >= 1:
                        S.wait_ge(svv.sem, um_mark[ci - 1])
                        Q.wait_ge(smm.sem, mm_mark[ci - 1])
                    for cx in range(2):
                        c = c0 + cx
                        dma("q", x2ch[par][:, :, cx * 512:(cx + 1) * 512],
                            a1_out[b][c, :, :].rearrange("(c2 p) t -> p c2 t", p=128))
                        dma("q", dtc[:, :, cx * 512:(cx + 1) * 512],
                            ag_out[b][c, 0:DTR, :].rearrange("(c2 p) t -> p c2 t", p=128))
                    Q.wait_ge(dq.sem, dq.val)
                    for n in range(NST):
                        dma("s", bq[par][:, n, :].rearrange("p (c t) -> p c t", c=2),
                            ag_out[b][c0:c0 + 2, DTR + n, :].partition_broadcast(128))
                    ld_mark[ci] = (dq.val, ds.val)

                def issue_cq(ci):
                    b, q4 = divmod(ci, 4)
                    c0 = 2 * q4
                    for n in range(NST):
                        dma("q", cq[:, n, :].rearrange("p (c t) -> p c t", c=2),
                            ag_out[b][c0:c0 + 2, DTR + NST + n, :].partition_broadcast(128))
                    Q.wait_ge(dq.sem, dq.val)
                    cq_mark[ci] = dq.val

                def issue_delta_mm(ci):
                    par = ci % 2
                    T.wait_ge(dq.sem, ld_mark[ci][0])
                    if ci >= 2:
                        T.wait_ge(svv.sem, um_mark[ci - 2])
                        T.wait_ge(sss.sem, sp_mark[ci - 2])
                    for ot in range(2):
                        for ch2 in range(2):
                            psDo = ps8[:, par * 4 + ot * 2 + ch2, :]
                            for kt in range(2):
                                mi = T.matmul(psDo, wdt[:, kt, ot * 128:(ot + 1) * 128],
                                              dtc[:, kt, ch2 * 512:(ch2 + 1) * 512],
                                              start=(kt == 0), stop=(kt == 1))
                    imark(smm, mi)
                    mm_mark[ci] = smm.val

                def issue_softplus(ci):
                    par = ci % 2
                    psD = ps8[:, par * 4:par * 4 + 4, :].rearrange("p (o c) t -> p o c t", o=2)
                    S.wait_ge(smm.sem, mm_mark[ci])
                    if ci >= 2:
                        S.wait_ge(svv.sem, vchunk.get(ci - 2, 0))
                    for ot in range(2):
                        mi = S.activation(dlt[par][:, ot, :], psD[:, ot, :, :], AF.Exp,
                                          scale=-1.0, bias=nbdt[:, ot:ot + 1])
                    schain(sss, S, mi)
                    imark(sss, S.activation(dlt[par][:], dlt[par][:], AF.Ln, bias=1.0))
                    sp_mark[ci] = sss.val

                issue_loads(0)
                issue_cq(0)
                issue_delta_mm(0)
                issue_softplus(0)

                it = 0
                for ci in range(NCH):
                    b, q4 = divmod(ci, 4)
                    par = ci % 2
                    psD = ps8[:, par * 4:par * 4 + 4, :].rearrange("p (o c) t -> p o c t", o=2)
                    V.wait_ge(sss.sem, sp_mark[ci])
                    V.wait_ge(dq.sem, ld_mark[ci][0])
                    V.wait_ge(ds.sem, ld_mark[ci][1])
                    for ot in range(2):
                        mi = V.scalar_tensor_tensor(dlt[par][:, ot, :], psD[:, ot, :, :],
                                                    bdt[:, ot:ot + 1], dlt[par][:, ot, :],
                                                    ALU.add, ALU.add)
                    schain(svv, V, mi)
                    schain(svv, V, V.tensor_tensor(x2ch[par][:], dlt[par][:],
                                                   x2ch[par][:], ALU.mult))
                    um_mark[ci] = svv.val
                    imark(svv, V.tensor_tensor(dbu[it % 2][:], x2ch[par][:, 0, :],
                                               bq[par][:, 0, :], ALU.mult))
                    dbu_marks.append(svv.val)
                    if ci + 1 < NCH:
                        issue_loads(ci + 1)
                        issue_delta_mm(ci + 1)
                    for dc in range(2):
                        acc_marks = []
                        for n in range(NST):
                            k = it
                            if dc == 0 and n == 0:
                                S.wait_ge(svv.sem, um_mark[ci])
                            if k >= 3:
                                S.wait_ge(svv.sem, scan_marks[k - 3])
                            imark(sss, S.activation(dab[k % 3][:], dlt[par][:, dc, :],
                                                    AF.Exp, scale=aneg[:, dc, n:n + 1]))
                            V.wait_ge(sss.sem, sss.val)
                            V.wait_ge(svv.sem, dbu_marks[k])
                            if q4 == 0:
                                mi = V.tensor_tensor_scan(hbt[k % 2][:], dab[k % 3][:],
                                                          dbu[k % 2][:], 0.0,
                                                          ALU.mult, ALU.add)
                            else:
                                mi = V.tensor_tensor_scan(hbt[k % 2][:], dab[k % 3][:],
                                                          dbu[k % 2][:],
                                                          stt[:, dc * 8 + n:dc * 8 + n + 1],
                                                          ALU.mult, ALU.add)
                            imark(svv, mi)
                            scan_marks.append(svv.val)
                            if n < NST - 1 or dc == 0:
                                dcn, nn = (dc, n + 1) if n < NST - 1 else (1, 0)
                                if k >= 1:
                                    V.wait_ge(svv.sem, scan_marks[k - 1])
                                mi = V.tensor_tensor(dbu[(k + 1) % 2][:],
                                                     x2ch[par][:, dcn, :],
                                                     bq[par][:, nn, :], ALU.mult)
                                imark(svv, mi)
                                dbu_marks.append(svv.val)
                            if n == 0 and dc == 0:
                                V.wait_ge(dq.sem, cq_mark[ci])
                            V.wait_ge(svv.sem, scan_marks[k])
                            imark(svv, V.tensor_tensor(hp[k % 2][:], hbt[k % 2][:],
                                                       cq[:, n, :], ALU.mult))
                            hp_marks.append(svv.val)
                            if n >= 1:
                                V.wait_ge(svv.sem,
                                          acc_marks[-1] if acc_marks else hp_marks[k - 1])
                                if n == 1:
                                    if (ci - 1, dc) in st_mark:
                                        V.wait_ge(dq.sem, st_mark[(ci - 1, dc)])
                                    mi = V.tensor_copy(ysc[dc][:], hp[(k - 1) % 2][:])
                                else:
                                    mi = V.tensor_tensor(ysc[dc][:], ysc[dc][:],
                                                         hp[(k - 1) % 2][:], ALU.add)
                                imark(svv, mi)
                                acc_marks.append(svv.val)
                            if q4 != 3:
                                imark(svv, V.tensor_copy(
                                    stt[:, dc * 8 + n:dc * 8 + n + 1],
                                    hbt[k % 2][:, QT - 1:QT]))
                            it += 1
                        if dc == 0 and ci + 1 < NCH:
                            issue_softplus(ci + 1)
                        V.wait_ge(svv.sem, acc_marks[-1])
                        imark(svv, V.tensor_tensor(ysc[dc][:], ysc[dc][:],
                                                   hp[(it - 1) % 2][:], ALU.add))
                        c0 = 2 * q4
                        Q.wait_ge(svv.sem, svv.val)
                        for cx in range(2):
                            c = c0 + cx
                            dma("q", a2_in[b][c, dc * 128:(dc + 1) * 128, :],
                                ysc[dc][:, cx * 512:(cx + 1) * 512])
                        Q.wait_ge(dq.sem, dq.val)
                        st_mark[(ci, dc)] = dq.val
                        if TRUNC >= 6 and q4 == 3 and dc == 1:
                            G.wait_ge(dq.sem, dq.val)
                            imark(scc, G.collective_compute(
                                "AllToAll", ALU.bypass,
                                ins=[a2_inT[b]], outs=[a2_outT[b]],
                                replica_groups=[list(range(NC))]))
                            a2_mark[b] = scc.val
                    vchunk[ci] = svv.val
                    if ci + 1 < NCH:
                        Q.wait_ge(svv.sem, vchunk[ci])
                        issue_cq(ci + 1)
                bar()

        es3 = ExitStack()
        sb3 = lambda n, s, d=F32: es3.enter_context(nc.sbuf_tensor(n, s, d))
        with es3:
            tn3 = dict(
                wbh=sb3("wbh3", [128, DC, 512], BF16),
                wb2=None,
                xlnb=sb3("xlnb3", [128, DC, TPC], BF16),
                mbuf=sb3("mbuf3", [128, TOK], BF16),
                hmk=sb3("hmk3", [128, LT, TOK], BF16),
            )
            smp3 = sb3("smp3", [128, 272])
            tn3["m8st"] = smp3[:, 16:272].rearrange("p (h t) -> p h t", h=4)
            tn3["mx8"] = smp3[:, 8:9]
            tn3["logsb"] = smp3[:, 0:8]
            smb3 = sb3("smb3", [128, 192], BF16)
            tn3["mtok"] = smb3[:, 0:64].rearrange("p (j h n) -> p j h n", j=LT, h=4)
            tn3["eye"] = smb3[:, 64:192]
            tn3["wbhB"] = sb3("wbhB3", [128, DC, 512], BF16)
            tn3["wb2"] = [tn3["wbh"], tn3["wbhB"]]
            wst3 = sb3("wst3", [128, DC, 512])
            tn3["wst"] = wst3
            ytl = (tn3["xlnb"].rearrange("p c t -> p (c t)").bitcast(F32)
                   .rearrange("p (c t) -> p c t", c=DC))
            tn3["at"] = sb3("at3", [128, DC, 128], BF16)
            tn3["bw"] = sb3("bw3", [128, D], BF16)
            tn3["e16"] = sb3("e16_s3", [NST, 128], BF16)
            for nm in ("mbc0", "rbc0", "mbc1", "rbc1"):
                lnt[nm] = sb3(nm + "3", [128, TOK])
            lnt["scr"] = [sb3(f"scr{p}3", [128, TOK]) for p in range(2)]

            with nc.Block():
                dma("g", tn3["eye"][:], eye_d)
                dma("g", tn3["e16"][:], e16_d)
                dma("g", tn3["at"][:], at_d.rearrange("c p e -> p c e"))
                dma("g", tn3["bw"][:], bw_d)
                tn3["gv"] = dg.val

                def resid(j):
                    b = j
                    Q.wait_ge(scc.sem, a2_mark[b])
                    Q.wait_ge(svv.sem, svv.val)
                    Q.wait_ge(smm.sem, smm.val)
                    for g8 in range(NC):
                        dma("q", ytl[:, 2 * g8:2 * g8 + 2, :],
                            a2_out[b][g8, :, :].rearrange("(c2 p) t -> p c2 t",
                                                          p=128))
                    Q.wait_ge(dq.sem, dq.val)
                    V.wait_ge(dq.sem, dq.val)
                    imark(svv, V.tensor_tensor(X[:, :, j * TOK:(j + 1) * TOK],
                                               X[:, :, j * TOK:(j + 1) * TOK],
                                               ytl[:], ALU.add))

                def pre3(j):
                    if j == 1 and TRUNC >= 7:
                        resid(1)

                def store_tile(j):
                    S.wait_ge(svv.sem, svv.val)
                    dma("s", out_d[j].rearrange("c p t -> p c t"),
                        X[:, :, j * TOK:(j + 1) * TOK])

                if TRUNC >= 7:
                    resid(0)
                if TRUNC >= 8:
                    ffn_sublayer(tn3, pre_tile=pre3, after_tile=store_tile)
                else:
                    dma("q", out_d.rearrange("l c p t -> p l c t"),
                        X[:].rearrange("p c (l t) -> p l c t", l=LT))
                for eng in ENG.values():
                    eng.wait_ge(dq.sem, dq.val)
                    eng.wait_ge(ds.sem, ds.val)

    return nc


_NC_CACHE = {}


def kernel(L0_attn_x, L0_attn_W_xproj, L0_attn_W_dt, L0_attn_b_dt, L0_attn_A_log,
           L0_attn_D_param, L0_ffn_W_base, L0_ffn_b_base, L0_ffn_router_W,
           L0_ffn_A, L0_ffn_Bw):
    x = np.ascontiguousarray(np.asarray(L0_attn_x, dtype=np.float32).reshape(NC * TPC, D).T)
    wbT = np.ascontiguousarray(np.asarray(L0_ffn_W_base, np.float32).T).reshape(DC, 128, D)
    bb = np.ascontiguousarray(np.asarray(L0_ffn_b_base, np.float32).reshape(DC, 128).T)
    rtT = np.ascontiguousarray(np.asarray(L0_ffn_router_W, np.float32).T).reshape(DC, 128, NST)
    atT = np.ascontiguousarray(np.asarray(L0_ffn_A, np.float32).reshape(128, D).T).reshape(DC, 128, 128)
    bwT = np.ascontiguousarray(np.transpose(np.asarray(L0_ffn_Bw, np.float32), (0, 2, 1)).reshape(128, D))
    wxT = np.zeros((D, 384), np.float32)
    wxT[:, 0:272] = np.asarray(L0_attn_W_xproj, np.float32).T
    wxT = wxT.reshape(DC, 128, 384)
    ones_col = np.ones((128, 128), np.float32)
    ones_row = np.ones((1, TOK), np.float32)
    eye = np.eye(128, dtype=np.float32)
    e16 = np.zeros((NST, 128), np.float32)
    for e in range(NST):
        e16[e, e * 16:(e + 1) * 16] = 1.0
    dparv = np.ascontiguousarray(np.asarray(L0_attn_D_param, np.float32).reshape(DC, 128).T)
    W_dt = np.asarray(L0_attn_W_dt, np.float32)
    b_dt = np.asarray(L0_attn_b_dt, np.float32)
    A_log = np.asarray(L0_attn_A_log, np.float32)

    key = os.environ.get("KTRUNC", "9")
    if key not in _NC_CACHE:
        _NC_CACHE[key] = build_program()
    nc = _NC_CACHE[key]

    in_maps = []
    for c in range(NC):
        cols = np.concatenate([np.arange((8 * j + c) * TOK, (8 * j + c + 1) * TOK)
                               for j in range(LT)])
        x0c = np.ascontiguousarray(
            x[:, cols].reshape(D, LT, TOK).transpose(1, 0, 2)).reshape(
            LT, DC, 128, TOK)
        wdtT = np.ascontiguousarray(W_dt[CH * c:CH * (c + 1), :].T).reshape(2, 128, CH)
        in_maps.append(dict(
            x0=x0c, wbT=wbT, bb=bb, routerT=rtT, AT=atT, BwT=bwT, WxpT=wxT,
            WdtT=wdtT,
            bdt=np.ascontiguousarray(b_dt[CH * c:CH * (c + 1)].reshape(2, 128).T),
            Alog=np.ascontiguousarray(
                A_log[CH * c:CH * (c + 1), :].reshape(2, 128, NST).transpose(1, 0, 2)),
            Dpar=dparv, ones_col=ones_col, ones_row=ones_row, eye=eye, E16=e16,
        ))
    ncore = int(os.environ.get("KCORES", str(NC)))
    res = run_bass_kernel_spmd(nc, in_maps[:ncore], core_ids=list(range(ncore)))
    out = np.zeros((D, NC * TPC), np.float32)
    for c in range(ncore):
        oc = res.results[c]["out"].reshape(LT, D, TOK)
        for j in range(LT):
            t0 = (8 * j + c) * TOK
            out[:, t0:t0 + TOK] = oc[j]
    return np.ascontiguousarray(out.T).reshape(2, 4096, D)


# version 20 (final)
# speedup vs baseline: 1.0426x; 1.0023x over previous
import os
os.environ.setdefault("TRN_ALLOW_ICI_ALL_TO_ALL", "1")

import numpy as np
from contextlib import ExitStack

import concourse.bass as bass
import concourse.mybir as mybir
from concourse.bass_utils import run_bass_kernel_spmd

F32 = mybir.dt.float32
BF16 = mybir.dt.bfloat16
AF = mybir.ActivationFunctionType
ALU = mybir.AluOpType
AX = mybir.AxisListType

NC = 8
D = 2048
DC = 16
TPC = 1024
LT = 2
TOK = 512
NB = 2
NST = 8
DTR = 256
CH = 256
HB = DTR + 16
QT = 1024
EPS = 1e-5


def build_program():
    TRUNC = int(os.environ.get("KTRUNC", "9"))
    nc = bass.Bass("TRN2", target_bir_lowering=False)

    def din(name, shape, dt=F32):
        return nc.dram_tensor(name, shape, dt, kind="ExternalInput").ap()

    x0_d = din("x0", [LT, DC, 128, TOK])
    wb_d = din("wbT", [DC, 128, D])
    bb_d = din("bb", [128, DC])
    rt_d = din("routerT", [DC, 128, NST])
    at_d = din("AT", [DC, 128, 128])
    bw_d = din("BwT", [128, D])
    wx_d = din("WxpT", [DC, 128, 384])
    wdt_d = din("WdtT", [2, 128, CH])
    bdt_d = din("bdt", [128, 2])
    alog_d = din("Alog", [128, 2, NST])
    dpar_d = din("Dpar", [128, DC])
    ones_col_d = din("ones_col", [128, 128])
    ones_row_d = din("ones_row", [1, TOK])
    eye_d = din("eye", [128, 128])
    e16_d = din("E16", [NST, 128])

    out_d = nc.dram_tensor("out", [LT, DC, 128, TOK], F32, kind="ExternalOutput").ap()

    ag_inT = nc.dram_tensor("ag_inT", [NB, HB, TOK], BF16, kind="Internal").ap()
    ag_outT = nc.dram_tensor("ag_outT", [NB, NC, HB, TOK], BF16, kind="Internal").ap()
    a1_inT = nc.dram_tensor("a1_inT", [NB, NC, CH, TOK], F32, kind="Internal").ap()
    a1_outT = nc.dram_tensor("a1_outT", [NB, NC, CH, TOK], F32, kind="Internal").ap()
    a2_inT = nc.dram_tensor("a2_inT", [NB, NC, CH, TOK], F32, kind="Internal").ap()
    a2_outT = nc.dram_tensor("a2_outT", [NB, NC, CH, TOK], F32, kind="Internal").ap()
    ag_in = [ag_inT[b] for b in range(NB)]
    ag_out = [ag_outT[b] for b in range(NB)]
    a1_in = [a1_inT[b] for b in range(NB)]
    a1_out = [a1_outT[b] for b in range(NB)]
    a2_in = [a2_inT[b] for b in range(NB)]
    a2_out = [a2_outT[b] for b in range(NB)]

    T, V, S, G, Q = nc.tensor, nc.vector, nc.scalar, nc.gpsimd, nc.sync
    ENG = {"t": T, "v": V, "s": S, "g": G, "q": Q}

    es = ExitStack()
    sb = lambda n, s, d=F32: es.enter_context(nc.sbuf_tensor(n, s, d))

    with es:
        X = sb("X", [128, DC, TPC])
        rt = sb("rt", [128, DC, NST], BF16)
        wdt = sb("wdt", [128, 2, CH], BF16)
        smf = sb("smf", [128, 80])
        bdt = smf[:, 0:2]
        nbdt = smf[:, 2:4]
        aneg = smf[:, 4:20].rearrange("p (c n) -> p c n", c=2)
        dpar = smf[:, 20:36]
        stt = smf[:, 36:52]
        hb = smf[:, 52:68]
        ones_c = sb("ones_c", [128, 128], BF16)
        onesf = sb("onesf", [128, 128])
        lnt = {}

        ps8 = es.enter_context(nc.psum_tensor("ps8", [128, 8, 512], F32))

        OWN = {}

        class Ctr:
            def __init__(self, name):
                self.sem = es.enter_context(nc.semaphore(name))
                self.val = 0
        bs = {e: Ctr(f"bar_{e}") for e in ENG}
        dq = Ctr("dq"); dg = Ctr("dg"); ds = Ctr("ds")
        smm = Ctr("smm"); svv = Ctr("svv"); sss = Ctr("sss"); sgg = Ctr("sgg")
        scc = Ctr("scc")
        DQ = {"q": dq, "g": dg, "s": ds}
        OWN.update({"t": smm, "v": svv, "s": sss, "g": sgg})

        def bar():
            for e in ENG:
                bs[e].val += 1
            for e, eng in ENG.items():
                if e in DQ:
                    eng.wait_ge(DQ[e].sem, DQ[e].val)
                if e in OWN:
                    eng.wait_ge(OWN[e].sem, OWN[e].val)
                eng.sem_inc(bs[e].sem, 1)
            for e, eng in ENG.items():
                for o in ENG:
                    if o != e:
                        eng.wait_ge(bs[o].sem, bs[o].val)

        def dma(k, out, in_):
            c = DQ[k]
            if k == "g":
                ENG[k].wait_ge(c.sem, c.val)
            c.val += 16
            ENG[k].dma_start(out=out, in_=in_).then_inc(c.sem, 16)

        def mark(ctr, eng):
            eng.nop().then_inc(ctr.sem, 1)
            ctr.val += 1

        def imark(ctr, inst):
            inst.then_inc(ctr.sem, 1)
            ctr.val += 1

        def schain(ctr, eng, inst):
            imark(ctr, inst)
            eng.wait_ge(ctr.sem, ctr.val)

        with nc.Block():
            dma("g", rt[:], rt_d.rearrange("c p e -> p c e"))
            dma("g", wdt[:], wdt_d.rearrange("c p m -> p c m"))
            dma("q", hb[:], bb_d)
            dma("g", ones_c[:], ones_col_d)
            dma("q", onesf[:], ones_col_d)
            dma("q", bdt[:], bdt_d)
            dma("q", dpar[:], dpar_d)
            dma("q", aneg[:], alog_d)
            S.wait_ge(dq.sem, dq.val)
            imark(sss, S.activation(hb[:], hb[:], AF.Copy, scale=0.5))
            imark(sss, S.activation(aneg[:], aneg[:], AF.Exp))
            V.wait_ge(sss.sem, sss.val)
            V.tensor_scalar(aneg[:], aneg[:], -1.0, None, ALU.mult)
            imark(svv, V.tensor_scalar(nbdt[:], bdt[:], -1.0, None, ALU.mult))
            bar()

        def ln_stats(j, dst_bf, bk, mbc, rbc, dqv=None):
            sq_lt = dst_bf[:, :, j * TOK:(j + 1) * TOK]
            vtm = mbc
            psSx = ps8[:, bk, :]
            psSq = ps8[:, bk + 1, :]
            if dqv is not None:
                S.wait_ge(dq.sem, dqv)
                T.wait_ge(dq.sem, dqv)
            S.wait_ge(svv.sem, svv.val)
            imark(sss, S.activation(sq_lt, X[:, :, j * TOK:(j + 1) * TOK], AF.Square))
            T.wait_ge(svv.sem, svv.val)
            T.wait_ge(sss.sem, sss.val)
            for kt in range(DC):
                T.matmul(psSx, onesf[:], X[:, kt, j * TOK:(j + 1) * TOK],
                         start=(kt == 0), stop=(kt == DC - 1))
            for kt in range(DC):
                mi = T.matmul(psSq, ones_c[:], sq_lt[:, kt, :],
                              start=(kt == 0), stop=(kt == DC - 1))
            imark(smm, mi)
            S.wait_ge(smm.sem, smm.val)
            S.activation(vtm[:], psSx, AF.Square, scale=1.0 / D)
            imark(sss, S.activation(rbc[:], psSq, AF.Copy, scale=1.0 / D))
            V.wait_ge(sss.sem, sss.val)
            imark(svv, V.scalar_tensor_tensor(rbc[:], rbc[:], EPS, vtm[:],
                                              ALU.add, ALU.subtract))
            S.wait_ge(svv.sem, svv.val)
            imark(sss, S.activation(mbc[:], psSx, AF.Copy, scale=1.0 / D))
            T.wait_ge(sss.sem, sss.val)
            schain(sss, S, S.activation(rbc[:], rbc[:], AF.Ln))
            imark(sss, S.activation(rbc[:], rbc[:], AF.Exp, scale=-0.5))

        def ln_apply(j, dst_bf, mbc, rbc, extra_ap=None, post_dc=None):
            scr = lnt["scr"]
            subm = {}
            usem = {}
            V.wait_ge(sss.sem, sss.val)
            V.wait_ge(svv.sem, svv.val)
            imark(svv, V.tensor_tensor(scr[0][:], X[:, 0, j * TOK:(j + 1) * TOK],
                                       mbc[:], ALU.subtract))
            subm[0] = svv.val
            for dc in range(DC):
                if dc + 1 < DC:
                    if dc >= 1:
                        V.wait_ge(svv.sem, usem[dc - 1])
                    imark(svv, V.tensor_tensor(scr[(dc + 1) % 2][:],
                                               X[:, dc + 1, j * TOK:(j + 1) * TOK],
                                               mbc[:], ALU.subtract))
                    subm[dc + 1] = svv.val
                V.wait_ge(svv.sem, subm[dc])
                mi = V.tensor_tensor(dst_bf[:, dc, j * TOK:(j + 1) * TOK],
                                     scr[dc % 2][:], rbc[:], ALU.mult)
                if extra_ap is not None:
                    mi = V.tensor_tensor(extra_ap(dc), scr[dc % 2][:], rbc[:],
                                         ALU.mult)
                imark(svv, mi)
                usem[dc] = svv.val
                if post_dc is not None:
                    post_dc(dc)
            S.wait_ge(svv.sem, svv.val)

        def ffn_sublayer(tn, pre_tile=None, after_tile=None):
            xlnb, m8st = tn["xlnb"], tn["m8st"]
            logsb = tn["logsb"]
            mtok, mbuf, hmk, mx8 = tn["mtok"], tn["mbuf"], tn["hmk"], tn["mx8"]
            wst, wb2 = tn["wst"], tn["wb2"]
            xt = tn.get("xt")
            psLog = ps8[:, 4, 0:NST]
            psT8 = ps8[0:NST, 5, 0:64].bitcast(BF16)
            psT8w = ps8[:, 5, 0:64]
            psH = ps8[:, 6, :]
            psM = ps8[:, 7, :]
            wb_src = wb_d.rearrange("c p o -> p c o")

            NU = 8
            ldm = {}
            cmark = {}
            qmm = {}
            st = {"nld": 0, "ncst": 0}

            def emit_load(k):
                if k >= 2:
                    Q.wait_ge(sss.sem, cmark[k - 2])
                qq = (k // 2) % 4
                dma("q", wst[:, :, (k % 2) * 256:(k % 2) * 256 + 256],
                    wb_src[:, :, qq * 512 + (k % 2) * 256:
                               qq * 512 + (k % 2) * 256 + 256])
                Q.wait_ge(dq.sem, dq.val)
                ldm[k] = dq.val

            def emit_cast(k):
                u, hh = divmod(k, 2)
                S.wait_ge(dq.sem, ldm[k])
                if u >= 2 and hh == 0:
                    S.wait_ge(smm.sem, qmm[u - 2])
                imark(sss, S.activation(wb2[u % 2][:, :, hh * 256:(hh + 1) * 256],
                                        wst[:, :, (k % 2) * 256:(k % 2) * 256 + 256],
                                        AF.Copy, scale=0.5))
                cmark[k] = sss.val

            def pump(n):
                n = min(n, 2 * NU)
                while st["ncst"] < n:
                    while st["nld"] < min(st["ncst"] + 2, 2 * NU):
                        emit_load(st["nld"]); st["nld"] += 1
                    emit_cast(st["ncst"]); st["ncst"] += 1
                while st["nld"] < min(n + 2, 2 * NU):
                    emit_load(st["nld"]); st["nld"] += 1

            if tn.get("war_svv") is not None:
                Q.wait_ge(svv.sem, tn["war_svv"])
            pump(0)

            ep_marks = []
            g_idx = 0
            for j in range(LT):
                if pre_tile is not None:
                    pre_tile(j)
                mbc, rbc = lnt[f"mbc{j}"], lnt[f"rbc{j}"]
                ln_stats(j, xlnb, 4, mbc, rbc, dqv=(xt[j] if xt else None))
                ln_apply(j, xlnb, mbc, rbc)
                am = svv.val
                T.wait_ge(svv.sem, am)
                T.wait_ge(sss.sem, sss.val)
                for half in range(4):
                    t0 = j * TOK + half * 128
                    for kt in range(DC):
                        mi = T.matmul(psLog, xlnb[:, kt, t0:t0 + 128],
                                      rt[:, kt, :], start=(kt == 0),
                                      stop=(kt == DC - 1))
                    imark(smm, mi)
                    S.wait_ge(smm.sem, smm.val)
                    S.wait_ge(svv.sem, svv.val)
                    imark(sss, S.copy(logsb[:], psLog))
                    T.wait_ge(sss.sem, sss.val)
                    V.wait_ge(sss.sem, sss.val)
                    schain(svv, V, V.reduce_max(mx8[:], logsb[:], axis=AX.X))
                    imark(svv, V.tensor_scalar(mtok[:, j, half, :], logsb[:],
                                               mx8[:], 1.0,
                                               ALU.is_equal, ALU.mult))
                if j == 0 and tn.get("gv") is not None:
                    T.wait_ge(dg.sem, tn["gv"])
                T.wait_ge(svv.sem, svv.val)
                for half in range(4):
                    imark(smm, T.matmul(psT8, mtok[:, j, half, :], tn["eye"][:],
                                        is_transpose=True, start=True, stop=True))
                    S.wait_ge(smm.sem, smm.val)
                    imark(sss, S.copy(m8st[:, half, :], psT8w))
                    T.wait_ge(sss.sem, sss.val)
                imark(smm, T.matmul(psM, tn["e16"][:],
                      m8st[0:NST, :, :].bitcast(BF16).rearrange("p h t -> p (h t)"),
                      start=True, stop=True))
                S.wait_ge(smm.sem, smm.val)
                S.wait_ge(svv.sem, svv.val)
                imark(sss, S.copy(mbuf[:], psM))
                T.wait_ge(sss.sem, sss.val)
                for kt in range(DC):
                    mi = T.matmul(psH, tn["at"][:, kt, :],
                                  xlnb[:, kt, j * TOK:(j + 1) * TOK],
                                  start=(kt == 0), stop=(kt == DC - 1))
                imark(smm, mi)
                V.wait_ge(smm.sem, smm.val)
                V.wait_ge(sss.sem, sss.val)
                imark(svv, V.tensor_tensor(hmk[:, j, :], psH, mbuf[:], ALU.mult))
                T.wait_ge(svv.sem, svv.val)
                for h in range(4):
                    u = j * 4 + h
                    pump(2 * u + 2)
                    wcur = wb2[u % 2]
                    T.wait_ge(sss.sem, cmark[2 * u + 1])
                    b0 = (g_idx % 2) * 4
                    if g_idx >= 2:
                        T.wait_ge(svv.sem, ep_marks[g_idx - 2])
                    for i in range(4):
                        og = h * 4 + i
                        psO = ps8[:, b0 + i, :]
                        for kt in range(DC):
                            T.matmul(psO, wcur[:, kt, i * 128:(i + 1) * 128],
                                     xlnb[:, kt, j * TOK:(j + 1) * TOK],
                                     start=(kt == 0), stop=False)
                        mi = T.matmul(psO, tn["bw"][:, og * 128:(og + 1) * 128],
                                      hmk[:, j, :], start=False, stop=True)
                    imark(smm, mi)
                    qmm[u] = smm.val
                    V.wait_ge(smm.sem, smm.val)
                    for i in range(4):
                        og = h * 4 + i
                        mi = V.scalar_tensor_tensor(
                            X[:, og, j * TOK:(j + 1) * TOK],
                            ps8[:, b0 + i, :], hb[:, og:og + 1],
                            X[:, og, j * TOK:(j + 1) * TOK],
                            ALU.add, ALU.add)
                    imark(svv, mi)
                    ep_marks.append(svv.val)
                    g_idx += 1
                    pump(2 * (u + 3))
                if after_tile is not None:
                    after_tile(j)
            bar()

        cc_mark = {}
        cc_mark = {}
        a2_mark = {}

        es1 = ExitStack()
        sb1 = lambda n, s, d=F32: es1.enter_context(nc.sbuf_tensor(n, s, d))
        with es1:
            tn1 = dict(
                wbh=sb1("wbh", [128, DC, 512], BF16),
                wb2=None,
                xlnb=sb1("xlnb", [128, DC, TPC], BF16),
                mbuf=sb1("mbuf", [128, TOK], BF16),
                hmk=sb1("hmk", [128, LT, TOK], BF16),
            )
            smp1 = sb1("smp1", [128, 272])
            tn1["m8st"] = smp1[:, 16:272].rearrange("p (h t) -> p h t", h=4)
            tn1["mx8"] = smp1[:, 8:9]
            tn1["logsb"] = smp1[:, 0:8]
            smb1 = sb1("smb1", [128, 192], BF16)
            tn1["mtok"] = smb1[:, 0:64].rearrange("p (j h n) -> p j h n", j=LT, h=4)
            tn1["eye"] = smb1[:, 64:192]
            tn1["wbhB"] = sb1("wbhB", [128, DC, 512], BF16)
            tn1["wb2"] = [tn1["wbh"], tn1["wbhB"]]
            wst1 = sb1("wst1", [128, DC, 512])
            tn1["wst"] = wst1
            xl2 = sb1("xl2", [128, 4, TOK])
            wx = sb1("wx", [128, DC, 384], BF16)
            tn1["at"] = sb1("at", [128, DC, 128], BF16)
            tn1["bw"] = sb1("bw", [128, D], BF16)
            tn1["e16"] = sb1("e16_s", [NST, 128], BF16)
            for nm in ("mbc0", "rbc0", "mbc1", "rbc1"):
                lnt[nm] = sb1(nm, [128, TOK])
            lnt["scr"] = [sb1(f"scr{p}", [128, TOK]) for p in range(2)]

            with nc.Block():
                xt = []
                for j in range(LT):
                    dma("q", X[:, :, j * TOK:(j + 1) * TOK],
                        x0_d[j].rearrange("c p t -> p c t"))
                    Q.wait_ge(dq.sem, dq.val)
                    xt.append(dq.val)
                tn1["xt"] = xt
                dma("g", tn1["eye"][:], eye_d)
                dma("g", tn1["e16"][:], e16_d)
                dma("g", tn1["at"][:], at_d.rearrange("c p e -> p c e"))
                dma("g", tn1["bw"][:], bw_d)
                tn1["gv"] = dg.val
                dma("g", wx[:], wx_d.rearrange("c p k -> p c k"))
                wxv = dg.val
                stg_marks = []
                cc_mark.clear()

                def ln2_tile(j):
                    b = j
                    pm = {}

                    def eslot(dc):
                        s = dc % 4
                        for mark, pair in reversed(stg_marks):
                            if s in pair:
                                V.wait_ge(dq.sem, mark)
                                break
                        return xl2[:, dc % 4, :]

                    def dterm(dc, b=b, j=j):
                        V.wait_ge(svv.sem, pm[dc])
                        imark(svv, V.scalar_tensor_tensor(
                            X[:, dc, j * TOK:(j + 1) * TOK],
                            xl2[:, dc % 4, :], dpar[:, dc:dc + 1],
                            X[:, dc, j * TOK:(j + 1) * TOK],
                            ALU.mult, ALU.add))
                        if dc % 2 == 1:
                            p0 = (dc - 1) % 4
                            Q.wait_ge(svv.sem, svv.val)
                            dma("q", a1_in[b][dc // 2, :, :]
                                .rearrange("(c p) t -> p c t", p=128),
                                xl2[:, p0:p0 + 2, :])
                            Q.wait_ge(dq.sem, dq.val)
                            stg_marks.append((dq.val, ((dc - 1) % 4, dc % 4)))

                    def post(dc):
                        pm[dc] = svv.val
                        if dc >= 1:
                            dterm(dc - 1)

                    ln_stats(j, tn1["xlnb"], 4, lnt[f"mbc{j}"], lnt[f"rbc{j}"])
                    ln_apply(j, tn1["xlnb"], lnt[f"mbc{j}"], lnt[f"rbc{j}"],
                             extra_ap=eslot, post_dc=post)
                    dterm(DC - 1)
                    T.wait_ge(svv.sem, svv.val)
                    T.wait_ge(sss.sem, sss.val)
                    T.wait_ge(dg.sem, wxv)
                    for ot in (0, 2, 1):
                        psXo = ps8[:, 4 + ot, :]
                        for kt in range(DC):
                            mi = T.matmul(psXo, wx[:, kt, ot * 128:(ot + 1) * 128],
                                          tn1["xlnb"][:, kt, j * TOK:(j + 1) * TOK],
                                          start=(kt == 0), stop=(kt == DC - 1))
                    imark(smm, mi)
                    xdbe = tn1["xlnb"][:, 0:3, j * TOK:(j + 1) * TOK]
                    S.wait_ge(smm.sem, smm.val)
                    imark(sss, S.copy(xdbe, ps8[:, 4:7, :]))
                    T.wait_ge(sss.sem, sss.val)
                    Q.wait_ge(sss.sem, sss.val)
                    dma("q", ag_in[b][0:DTR, :].rearrange("(c p) t -> p c t", p=128),
                        xdbe[:, 0:2, :])
                    dma("q", ag_in[b][DTR:HB, :], xdbe[0:16, 2, :])
                    Q.wait_ge(dq.sem, dq.val)
                    V.wait_ge(dq.sem, dq.val)
                    S.wait_ge(dq.sem, dq.val)
                    if TRUNC >= 4:
                        G.wait_ge(dq.sem, dq.val)
                        imark(scc, G.collective_compute(
                            "AllGather", ALU.bypass,
                            ins=[ag_inT[b]], outs=[ag_outT[b]],
                            replica_groups=[list(range(NC))]))
                        imark(scc, G.collective_compute(
                            "AllToAll", ALU.bypass,
                            ins=[a1_inT[b]], outs=[a1_outT[b]],
                            replica_groups=[list(range(NC))]))
                        cc_mark[b] = scc.val

                if TRUNC >= 2:
                    ffn_sublayer(tn1,
                                 after_tile=(ln2_tile if TRUNC >= 3 else None))

        es2 = ExitStack()
        sb2 = lambda n, s, d=F32: es2.enter_context(nc.sbuf_tensor(n, s, d))
        with es2:
            x2ch = [sb2(f"x2ch{p}", [128, 2, QT]) for p in range(2)]
            dtc = sb2("dtc", [128, 2, QT], BF16)
            dlt = [sb2(f"dlt{p}", [128, 2, QT]) for p in range(2)]
            dab = [sb2(f"dab{p}", [128, QT]) for p in range(3)]
            dbu = [sb2(f"dbu{p}", [128, QT]) for p in range(2)]
            hbt = [sb2(f"hbt{p}", [128, QT]) for p in range(2)]
            hp = [sb2(f"hp{p}", [128, QT]) for p in range(2)]
            ysc = [sb2(f"ysc{p}", [128, QT]) for p in range(2)]
            bq = [sb2(f"bq{p}", [128, NST, QT], BF16) for p in range(2)]
            cq = sb2("cq", [128, NST, QT], BF16)

            with nc.Block():
                bar()

                NCH = 8 if TRUNC >= 5 else 0
                ld_mark = {}
                cq_mark = {}
                mm_mark = {}
                sp_mark = {}
                st_mark = {}
                um_mark = {}
                vchunk = {}
                scan_marks = []
                hc_marks = []
                dbu_marks = []
                hp_marks = []

                def issue_loads(ci):
                    b, q4 = divmod(ci, 4)
                    par = ci % 2
                    c0 = 2 * q4
                    if TRUNC >= 4 and (ci == 0 or ci == 4):
                        Q.wait_ge(scc.sem, cc_mark[b])
                        S.wait_ge(scc.sem, cc_mark[b])
                    if ci >= 2 and (ci - 2) in vchunk:
                        Q.wait_ge(svv.sem, vchunk[ci - 2])
                    if ci >= 1:
                        S.wait_ge(svv.sem, um_mark[ci - 1])
                        Q.wait_ge(smm.sem, mm_mark[ci - 1])
                    for cx in range(2):
                        c = c0 + cx
                        dma("q", x2ch[par][:, :, cx * 512:(cx + 1) * 512],
                            a1_out[b][c, :, :].rearrange("(c2 p) t -> p c2 t", p=128))
                        dma("q", dtc[:, :, cx * 512:(cx + 1) * 512],
                            ag_out[b][c, 0:DTR, :].rearrange("(c2 p) t -> p c2 t", p=128))
                    Q.wait_ge(dq.sem, dq.val)
                    for n in range(NST):
                        dma("s", bq[par][:, n, :].rearrange("p (c t) -> p c t", c=2),
                            ag_out[b][c0:c0 + 2, DTR + n, :].partition_broadcast(128))
                    ld_mark[ci] = (dq.val, ds.val)

                def issue_cq(ci):
                    b, q4 = divmod(ci, 4)
                    c0 = 2 * q4
                    for n in range(NST):
                        dma("q", cq[:, n, :].rearrange("p (c t) -> p c t", c=2),
                            ag_out[b][c0:c0 + 2, DTR + NST + n, :].partition_broadcast(128))
                    Q.wait_ge(dq.sem, dq.val)
                    cq_mark[ci] = dq.val

                def issue_delta_mm(ci):
                    par = ci % 2
                    T.wait_ge(dq.sem, ld_mark[ci][0])
                    if ci >= 2:
                        T.wait_ge(svv.sem, um_mark[ci - 2])
                        T.wait_ge(sss.sem, sp_mark[ci - 2])
                    for ot in range(2):
                        for ch2 in range(2):
                            psDo = ps8[:, par * 4 + ot * 2 + ch2, :]
                            for kt in range(2):
                                mi = T.matmul(psDo, wdt[:, kt, ot * 128:(ot + 1) * 128],
                                              dtc[:, kt, ch2 * 512:(ch2 + 1) * 512],
                                              start=(kt == 0), stop=(kt == 1))
                    imark(smm, mi)
                    mm_mark[ci] = smm.val

                def issue_softplus(ci):
                    par = ci % 2
                    psD = ps8[:, par * 4:par * 4 + 4, :].rearrange("p (o c) t -> p o c t", o=2)
                    S.wait_ge(smm.sem, mm_mark[ci])
                    if ci >= 2:
                        S.wait_ge(svv.sem, vchunk.get(ci - 2, 0))
                    for ot in range(2):
                        mi = S.activation(dlt[par][:, ot, :], psD[:, ot, :, :], AF.Exp,
                                          scale=-1.0, bias=nbdt[:, ot:ot + 1])
                    schain(sss, S, mi)
                    imark(sss, S.activation(dlt[par][:], dlt[par][:], AF.Ln, bias=1.0))
                    sp_mark[ci] = sss.val

                issue_loads(0)
                issue_cq(0)
                issue_delta_mm(0)
                issue_softplus(0)

                it = 0
                for ci in range(NCH):
                    b, q4 = divmod(ci, 4)
                    par = ci % 2
                    psD = ps8[:, par * 4:par * 4 + 4, :].rearrange("p (o c) t -> p o c t", o=2)
                    V.wait_ge(sss.sem, sp_mark[ci])
                    V.wait_ge(dq.sem, ld_mark[ci][0])
                    V.wait_ge(ds.sem, ld_mark[ci][1])
                    for ot in range(2):
                        mi = V.scalar_tensor_tensor(dlt[par][:, ot, :], psD[:, ot, :, :],
                                                    bdt[:, ot:ot + 1], dlt[par][:, ot, :],
                                                    ALU.add, ALU.add)
                    schain(svv, V, mi)
                    schain(svv, V, V.tensor_tensor(x2ch[par][:], dlt[par][:],
                                                   x2ch[par][:], ALU.mult))
                    um_mark[ci] = svv.val
                    imark(svv, V.tensor_tensor(dbu[it % 2][:], x2ch[par][:, 0, :],
                                               bq[par][:, 0, :], ALU.mult))
                    dbu_marks.append(svv.val)
                    if ci + 1 < NCH:
                        issue_loads(ci + 1)
                        issue_delta_mm(ci + 1)
                    for dc in range(2):
                        acc_marks = []
                        for n in range(NST):
                            k = it
                            if dc == 0 and n == 0:
                                S.wait_ge(svv.sem, um_mark[ci])
                            if k >= 3:
                                S.wait_ge(svv.sem, scan_marks[k - 3])
                            imark(sss, S.activation(dab[k % 3][:], dlt[par][:, dc, :],
                                                    AF.Exp, scale=aneg[:, dc, n:n + 1]))
                            V.wait_ge(sss.sem, sss.val)
                            V.wait_ge(svv.sem, dbu_marks[k])
                            if q4 == 0:
                                mi = V.tensor_tensor_scan(hbt[k % 2][:], dab[k % 3][:],
                                                          dbu[k % 2][:], 0.0,
                                                          ALU.mult, ALU.add)
                            else:
                                mi = V.tensor_tensor_scan(hbt[k % 2][:], dab[k % 3][:],
                                                          dbu[k % 2][:],
                                                          stt[:, dc * 8 + n:dc * 8 + n + 1],
                                                          ALU.mult, ALU.add)
                            imark(svv, mi)
                            scan_marks.append(svv.val)
                            if n < NST - 1 or dc == 0:
                                dcn, nn = (dc, n + 1) if n < NST - 1 else (1, 0)
                                if k >= 1:
                                    V.wait_ge(svv.sem, scan_marks[k - 1])
                                mi = V.tensor_tensor(dbu[(k + 1) % 2][:],
                                                     x2ch[par][:, dcn, :],
                                                     bq[par][:, nn, :], ALU.mult)
                                imark(svv, mi)
                                dbu_marks.append(svv.val)
                            if n == 0 and dc == 0:
                                V.wait_ge(dq.sem, cq_mark[ci])
                            V.wait_ge(svv.sem, scan_marks[k])
                            imark(svv, V.tensor_tensor(hp[k % 2][:], hbt[k % 2][:],
                                                       cq[:, n, :], ALU.mult))
                            hp_marks.append(svv.val)
                            if dc == 1 and n == NST - 1 and ci + 1 < NCH:
                                Q.wait_ge(svv.sem, hp_marks[-1])
                                issue_cq(ci + 1)
                            if n >= 1:
                                V.wait_ge(svv.sem,
                                          acc_marks[-1] if acc_marks else hp_marks[k - 1])
                                if n == 1:
                                    if (ci - 1, dc) in st_mark:
                                        V.wait_ge(dq.sem, st_mark[(ci - 1, dc)])
                                    mi = V.tensor_copy(ysc[dc][:], hp[(k - 1) % 2][:])
                                else:
                                    mi = V.tensor_tensor(ysc[dc][:], ysc[dc][:],
                                                         hp[(k - 1) % 2][:], ALU.add)
                                imark(svv, mi)
                                acc_marks.append(svv.val)
                            if q4 != 3:
                                imark(svv, V.tensor_copy(
                                    stt[:, dc * 8 + n:dc * 8 + n + 1],
                                    hbt[k % 2][:, QT - 1:QT]))
                            it += 1
                        if dc == 0 and ci + 1 < NCH:
                            issue_softplus(ci + 1)
                        V.wait_ge(svv.sem, acc_marks[-1])
                        imark(svv, V.tensor_tensor(ysc[dc][:], ysc[dc][:],
                                                   hp[(it - 1) % 2][:], ALU.add))
                        c0 = 2 * q4
                        Q.wait_ge(svv.sem, svv.val)
                        for cx in range(2):
                            c = c0 + cx
                            dma("q", a2_in[b][c, dc * 128:(dc + 1) * 128, :],
                                ysc[dc][:, cx * 512:(cx + 1) * 512])
                        Q.wait_ge(dq.sem, dq.val)
                        st_mark[(ci, dc)] = dq.val
                        if TRUNC >= 6 and q4 == 3 and dc == 1:
                            G.wait_ge(dq.sem, dq.val)
                            imark(scc, G.collective_compute(
                                "AllToAll", ALU.bypass,
                                ins=[a2_inT[b]], outs=[a2_outT[b]],
                                replica_groups=[list(range(NC))]))
                            a2_mark[b] = scc.val
                    vchunk[ci] = svv.val
                bar()

        es3 = ExitStack()
        sb3 = lambda n, s, d=F32: es3.enter_context(nc.sbuf_tensor(n, s, d))
        with es3:
            tn3 = dict(
                wbh=sb3("wbh3", [128, DC, 512], BF16),
                wb2=None,
                xlnb=sb3("xlnb3", [128, DC, TPC], BF16),
                mbuf=sb3("mbuf3", [128, TOK], BF16),
                hmk=sb3("hmk3", [128, LT, TOK], BF16),
            )
            smp3 = sb3("smp3", [128, 272])
            tn3["m8st"] = smp3[:, 16:272].rearrange("p (h t) -> p h t", h=4)
            tn3["mx8"] = smp3[:, 8:9]
            tn3["logsb"] = smp3[:, 0:8]
            smb3 = sb3("smb3", [128, 192], BF16)
            tn3["mtok"] = smb3[:, 0:64].rearrange("p (j h n) -> p j h n", j=LT, h=4)
            tn3["eye"] = smb3[:, 64:192]
            tn3["wbhB"] = sb3("wbhB3", [128, DC, 512], BF16)
            tn3["wb2"] = [tn3["wbh"], tn3["wbhB"]]
            wst3 = sb3("wst3", [128, DC, 512])
            tn3["wst"] = wst3
            ytl = (tn3["xlnb"].rearrange("p c t -> p (c t)").bitcast(F32)
                   .rearrange("p (c t) -> p c t", c=DC))
            tn3["at"] = sb3("at3", [128, DC, 128], BF16)
            tn3["bw"] = sb3("bw3", [128, D], BF16)
            tn3["e16"] = sb3("e16_s3", [NST, 128], BF16)
            for nm in ("mbc0", "rbc0", "mbc1", "rbc1"):
                lnt[nm] = sb3(nm + "3", [128, TOK])
            lnt["scr"] = [sb3(f"scr{p}3", [128, TOK]) for p in range(2)]

            with nc.Block():
                dma("g", tn3["eye"][:], eye_d)
                dma("g", tn3["e16"][:], e16_d)
                dma("g", tn3["at"][:], at_d.rearrange("c p e -> p c e"))
                dma("g", tn3["bw"][:], bw_d)
                tn3["gv"] = dg.val

                def resid(j):
                    b = j
                    Q.wait_ge(scc.sem, a2_mark[b])
                    Q.wait_ge(svv.sem, svv.val)
                    Q.wait_ge(smm.sem, smm.val)
                    for g8 in range(NC):
                        dma("q", ytl[:, 2 * g8:2 * g8 + 2, :],
                            a2_out[b][g8, :, :].rearrange("(c2 p) t -> p c2 t",
                                                          p=128))
                    Q.wait_ge(dq.sem, dq.val)
                    V.wait_ge(dq.sem, dq.val)
                    imark(svv, V.tensor_tensor(X[:, :, j * TOK:(j + 1) * TOK],
                                               X[:, :, j * TOK:(j + 1) * TOK],
                                               ytl[:], ALU.add))

                def pre3(j):
                    if j == 1 and TRUNC >= 7:
                        resid(1)

                def store_tile(j):
                    S.wait_ge(svv.sem, svv.val)
                    dma("s", out_d[j].rearrange("c p t -> p c t"),
                        X[:, :, j * TOK:(j + 1) * TOK])

                if TRUNC >= 7:
                    resid(0)
                if TRUNC >= 8:
                    ffn_sublayer(tn3, pre_tile=pre3, after_tile=store_tile)
                else:
                    dma("q", out_d.rearrange("l c p t -> p l c t"),
                        X[:].rearrange("p c (l t) -> p l c t", l=LT))
                for eng in ENG.values():
                    eng.wait_ge(dq.sem, dq.val)
                    eng.wait_ge(ds.sem, ds.val)

    return nc


_NC_CACHE = {}


def kernel(L0_attn_x, L0_attn_W_xproj, L0_attn_W_dt, L0_attn_b_dt, L0_attn_A_log,
           L0_attn_D_param, L0_ffn_W_base, L0_ffn_b_base, L0_ffn_router_W,
           L0_ffn_A, L0_ffn_Bw):
    x = np.ascontiguousarray(np.asarray(L0_attn_x, dtype=np.float32).reshape(NC * TPC, D).T)
    wbT = np.ascontiguousarray(np.asarray(L0_ffn_W_base, np.float32).T).reshape(DC, 128, D)
    bb = np.ascontiguousarray(np.asarray(L0_ffn_b_base, np.float32).reshape(DC, 128).T)
    rtT = np.ascontiguousarray(np.asarray(L0_ffn_router_W, np.float32).T).reshape(DC, 128, NST)
    atT = np.ascontiguousarray(np.asarray(L0_ffn_A, np.float32).reshape(128, D).T).reshape(DC, 128, 128)
    bwT = np.ascontiguousarray(np.transpose(np.asarray(L0_ffn_Bw, np.float32), (0, 2, 1)).reshape(128, D))
    wxT = np.zeros((D, 384), np.float32)
    wxT[:, 0:272] = np.asarray(L0_attn_W_xproj, np.float32).T
    wxT = wxT.reshape(DC, 128, 384)
    ones_col = np.ones((128, 128), np.float32)
    ones_row = np.ones((1, TOK), np.float32)
    eye = np.eye(128, dtype=np.float32)
    e16 = np.zeros((NST, 128), np.float32)
    for e in range(NST):
        e16[e, e * 16:(e + 1) * 16] = 1.0
    dparv = np.ascontiguousarray(np.asarray(L0_attn_D_param, np.float32).reshape(DC, 128).T)
    W_dt = np.asarray(L0_attn_W_dt, np.float32)
    b_dt = np.asarray(L0_attn_b_dt, np.float32)
    A_log = np.asarray(L0_attn_A_log, np.float32)

    key = os.environ.get("KTRUNC", "9")
    if key not in _NC_CACHE:
        _NC_CACHE[key] = build_program()
    nc = _NC_CACHE[key]

    in_maps = []
    for c in range(NC):
        cols = np.concatenate([np.arange((8 * j + c) * TOK, (8 * j + c + 1) * TOK)
                               for j in range(LT)])
        x0c = np.ascontiguousarray(
            x[:, cols].reshape(D, LT, TOK).transpose(1, 0, 2)).reshape(
            LT, DC, 128, TOK)
        wdtT = np.ascontiguousarray(W_dt[CH * c:CH * (c + 1), :].T).reshape(2, 128, CH)
        in_maps.append(dict(
            x0=x0c, wbT=wbT, bb=bb, routerT=rtT, AT=atT, BwT=bwT, WxpT=wxT,
            WdtT=wdtT,
            bdt=np.ascontiguousarray(b_dt[CH * c:CH * (c + 1)].reshape(2, 128).T),
            Alog=np.ascontiguousarray(
                A_log[CH * c:CH * (c + 1), :].reshape(2, 128, NST).transpose(1, 0, 2)),
            Dpar=dparv, ones_col=ones_col, ones_row=ones_row, eye=eye, E16=e16,
        ))
    ncore = int(os.environ.get("KCORES", str(NC)))
    res = run_bass_kernel_spmd(nc, in_maps[:ncore], core_ids=list(range(ncore)))
    out = np.zeros((D, NC * TPC), np.float32)
    for c in range(ncore):
        oc = res.results[c]["out"].reshape(LT, D, TOK)
        for j in range(LT):
            t0 = (8 * j + c) * TOK
            out[:, t0:t0 + TOK] = oc[j]
    return np.ascontiguousarray(out.T).reshape(2, 4096, D)
